# version 12
# baseline (speedup 1.0000x reference)
import numpy as np
import ml_dtypes
import concourse.bass as bass
import concourse.mybir as mybir
from concourse.bass_utils import run_bass_kernel_spmd
from contextlib import ExitStack
from concourse.bass import broadcast_tensor_aps

F32 = mybir.dt.float32
BF16 = mybir.dt.bfloat16
OP = mybir.AluOpType
AF = mybir.ActivationFunctionType
AX = mybir.AxisListType

N_CORES = 8
D = 2048
NT = 16
TQ = 512
TH = 1024
NE = 8
EPS = 1e-5
CLIP = 7.0
INV_D = 1.0 / D


class Ctr:
    def __init__(self, name):
        self.name = name
        self.n = 0
        self.sem = None


def build_nc(debug=False):
    nc = bass.Bass("TRN2", target_bir_lowering=False)

    xT = nc.dram_tensor("xT", [NT, 128, 2048], F32, kind="ExternalInput").ap()
    wr = nc.dram_tensor("wr", [NT, 128, 2048], BF16, kind="ExternalInput").ap()
    wk = nc.dram_tensor("wk", [NT, 128, 2048], BF16, kind="ExternalInput").ap()
    wv = nc.dram_tensor("wv", [NT, 128, 2048], BF16, kind="ExternalInput").ap()
    wo = nc.dram_tensor("wo", [NT, 128, 2048], BF16, kind="ExternalInput").ap()
    wg = nc.dram_tensor("wg", [NE, NT, 128, 2048], BF16, kind="ExternalInput").ap()
    wu = nc.dram_tensor("wu", [NE, NT, 128, 2048], BF16, kind="ExternalInput").ap()
    wd = nc.dram_tensor("wd", [NE, NT, 128, 2048], BF16, kind="ExternalInput").ap()
    wrf = nc.dram_tensor("wrf", [128, 128], BF16, kind="ExternalInput").ap()
    tmk = nc.dram_tensor("tmk", [128, 16], F32, kind="ExternalInput").ap()
    tmv = nc.dram_tensor("tmv", [128, 16], F32, kind="ExternalInput").ap()
    tmr = nc.dram_tensor("tmr", [128, 16], F32, kind="ExternalInput").ap()
    td = nc.dram_tensor("td", [128, 16], F32, kind="ExternalInput").ap()
    tf = nc.dram_tensor("tf", [128, 16], F32, kind="ExternalInput").ap()
    ident = nc.dram_tensor("ident", [128, 128], F32, kind="ExternalInput").ap()
    onesd = nc.dram_tensor("onesd", [128, 128], F32, kind="ExternalInput").ap()
    y = nc.dram_tensor("y", [2, NT, 128, TH], F32, kind="ExternalOutput").ap()
    if debug:
        dbg_xk = nc.dram_tensor("dbg_xk", [128, NT * TQ * 3], BF16, kind="ExternalOutput").ap()
        dbg_wkv = nc.dram_tensor("dbg_wkv", [128, NT * TQ], BF16, kind="ExternalOutput").ap()
        dbg_x1 = nc.dram_tensor("dbg_x1", [128, NT * TH], F32, kind="ExternalOutput").ap()
        dbg_ln2 = nc.dram_tensor("dbg_ln2", [128, NT * TH], BF16, kind="ExternalOutput").ap()
        dbg_wr = nc.dram_tensor("dbg_wr", [8, TH], F32, kind="ExternalOutput").ap()
        dbg_lg = nc.dram_tensor("dbg_lg", [8, TH], F32, kind="ExternalOutput").ap()
        dbg_sm = nc.dram_tensor("dbg_sm", [128, 96], F32, kind="ExternalOutput").ap()
        dbg_z = nc.dram_tensor("dbg_z", [128, NT * TH], BF16, kind="ExternalOutput").ap()

    es = ExitStack()
    sb = lambda name, shape, dt=F32: es.enter_context(nc.sbuf_tensor(name, shape, dt))

    x1 = sb("x1", [128, NT * TH])
    bigA = sb("bigA", [128, 2 * NT * TH], BF16)
    xk_v = bigA[:, 0 * NT * TQ: 1 * NT * TQ]
    xv_v = bigA[:, 1 * NT * TQ: 2 * NT * TQ]
    xr_v = bigA[:, 2 * NT * TQ: 3 * NT * TQ]
    wkv_v = bigA[:, 3 * NT * TQ: 4 * NT * TQ]
    ln2_v = bigA[:, 0: NT * TH]
    z_v = bigA[:, NT * TH: 2 * NT * TH]

    slab = [sb(f"slab{i}", [128, 2048], BF16) for i in range(6)]

    arena = sb("arena", [128, 5760])
    r_buf = [arena[:, 0:512], arena[:, 512:1024]]
    k_sb = [arena[:, 1024:1536], arena[:, 1536:2048]]
    kv_b = arena[:, 2048:2560]
    S_b = arena[:, 2560:3073]
    pre_b = arena[:, 3088:3600]
    lnx = arena[:, 3600:4113]
    d_b = arena[:, 4128:4640]
    wtil = arena[:, 4640:5152]
    lg_sb = arena[0:8, 0:1024]
    wr_sb = arena[0:8, 1024:2048]
    we_row = arena[0:1, 2048:3072]
    we_b = arena[:, 3072:4096]
    gs_b = [arena[:, 4640:5152], arena[:, 5152:5664]]
    w8 = arena[:, 5664:5672]
    l8 = arena[:, 5672:5680]
    eq1_b = arena[:, 5680:5688]
    eq2_b = arena[:, 5688:5696]
    t8 = arena[:, 5696:5704]
    m1_b = arena[:, 5704:5705]
    m2_b = arena[:, 5712:5713]
    p1_b = arena[:, 5720:5721]
    p2_b = arena[:, 5728:5729]
    m_row = sb("m_row", [1, TH])
    s_row = sb("s_row", [1, TH])
    sq_b = [sb(f"sq_b{i}", [128, TQ]) for i in range(2)]
    sq_b2 = [sq_b[0], sq_b[1], sb("sq_b2x", [128, TQ])]
    rows_sb = sb("rows_sb", [1, TH])
    mean_sb = sb("mean_sb", [128, TQ])
    rstd_sb = sb("rstd_sb", [128, TQ])
    we_row2 = sb("we_row2", [1, TH])
    we_b2 = sb("we_b2", [128, TH])

    ones_sb = sb("ones_sb", [128, 128])
    id_sb = sb("id_sb", [128, 128])
    wrf_sb = sb("wrf_sb", [128, 128], BF16)
    tmk_c = sb("tmk_c", [128, 16])
    tmv_c = sb("tmv_c", [128, 16])
    tmr_c = sb("tmr_c", [128, 16])
    td_c = sb("td_c", [128, 16])
    tf_c = sb("tf_c", [128, 16])
    w_c = sb("w_c", [128, 16])
    state = sb("state", [128, 16])
    bnd = sb("bnd", [128, 16])
    flc = sb("flc", [1, 8])

    ps = [es.enter_context(nc.psum_tensor(f"ps{i}", [128, 512], F32)) for i in range(8)]

    ssy = Ctr("ssy")
    sgd = Ctr("sgd")
    sgp = Ctr("sgp")
    smm = Ctr("smm")
    sv = Ctr("sv")
    sact = Ctr("sact")
    ssc = Ctr("ssc")
    ctrs = [ssy, sgd, sgp, smm, sv, sact, ssc]

    plan = {"sync": [], "tensor": [], "vector": [], "scalar": [], "gpsimd": []}
    _self_init = True

    def em(eng, f):
        plan[eng].append(f)

    def wait(eng, ctr, n):
        if n <= 0:
            return
        em(eng, lambda e, c=ctr, k=n: e.wait_ge(c.sem, k))

    SETTLE = 384

    def wait_settle(eng, ctr, n):
        if n <= 0:
            return
        em(eng, lambda e, c=ctr, k=n: e.wait_ge(c.sem, k))
        em(eng, lambda e: e.nop(cycle_cnt=SETTLE))

    _self = {"vector": sv, "scalar": sact, "gpsimd": sgp}

    def inc(eng, ctr, fmake, k=1):
        sc = _self.get(eng)
        if sc is not None and sc.n > 0 and ctr is sc:
            em(eng, lambda e, c=sc, kk=sc.n: e.wait_ge(c.sem, kk))
        ctr.n += k
        em(eng, lambda e, c=ctr, kk=k, fm=fmake: fm(e).then_inc(c.sem, kk))
        return ctr.n

    def just(eng, fmake):
        if eng == "vector":
            return inc("vector", sv, fmake)
        if eng == "scalar":
            return inc("scalar", sact, fmake)
        em(eng, lambda e, fm=fmake: fm(e))

    def act_flush():
        return inc("scalar", sact, lambda e: e.activation(flc[0:1, 0:1],
                                                          flc[0:1, 0:1], AF.Copy))

    slot_free_at = [0] * 6
    slot_dma = [(ssy, 0)] * 6

    def load_slab(queue, dram_slab, slot):
        qc = ssy if queue == "sync" else sgd
        wait(queue, smm, slot_free_at[slot])
        n = inc(queue, qc,
                lambda e, d=dram_slab, s=slot:
                e.dma_start(out=slab[s][:, :], in_=d), 16)
        slot_dma[slot] = (qc, n)

    def wait_slab(eng, slot):
        qc, n = slot_dma[slot]
        wait(eng, qc, n)

    for src, dst in [(ident, id_sb), (onesd, ones_sb),
                     (tmk, tmk_c), (tmv, tmv_c), (tmr, tmr_c),
                     (td, td_c), (tf, tf_c)]:
        inc("sync", ssy, lambda e, s=src, dd=dst: e.dma_start(out=dd[:, :], in_=s), 16)
    consts_n = ssy.n
    wrf_n = inc("gpsimd", sgd, lambda e: e.dma_start(out=wrf_sb[:, :], in_=wrf), 16)
    inc("gpsimd", sgp, lambda e: e.memset(state[:, :], 0.0))
    memset_n = inc("gpsimd", sgp, lambda e: e.memset(bnd[:, :], 0.0))
    wait("scalar", ssy, consts_n)
    just("scalar", lambda e: e.activation(w_c[:, :], td_c[:, :], AF.Exp))
    inc("scalar", sact, lambda e: e.activation(
        w_c[:, :], w_c[:, :], AF.Exp, scale=-1.0))
    wcol_n = act_flush()

    out_dma_done = [0, 0]
    stats_done = [0, 0, 0, 0]
    premix = {}

    def emit_quarter(q, prefetch_next=False):
        qc0 = (q % 2) * TQ
        premixed = premix.get("for_q") == q

        if not premixed:
            if q == 2:
                wait("sync", sgd, out_dma_done[0])
            xln = []
            for ht in range(NT):
                n = inc("sync", ssy,
                        lambda e, h=ht: e.dma_start(
                            out=x1[:, h * TH + qc0: h * TH + qc0 + TQ],
                            in_=xT[h][:, q * TQ:(q + 1) * TQ]), 16)
                xln.append(n)

            wait("vector", smm, smm.n)
            wait("vector", sact, sact.n)
            wait("tensor", sv, sv.n)
            wait("tensor", sact, sact.n)
            if q == 0:
                wait("vector", sgp, memset_n)

            for ht in range(NT):
                wait("tensor", ssy, xln[ht])
                if ht < NT - 1:
                    just("tensor", lambda e, h=ht, s=(ht == 0):
                         e.matmul(ps[6][0:1, :], ones_sb[:, 0:1],
                                  x1[:, h * TH + qc0: h * TH + qc0 + TQ],
                                  start=s, stop=False))
                else:
                    inc("tensor", smm, lambda e, h=ht:
                        e.matmul(ps[6][0:1, :], ones_sb[:, 0:1],
                                 x1[:, h * TH + qc0: h * TH + qc0 + TQ],
                                 start=False, stop=True))
            sqmm = []
            for ht in range(NT):
                wait("vector", ssy, xln[ht])
                if ht >= 2:
                    wait("vector", smm, sqmm[ht - 2])
                n = inc("vector", sv, lambda e, h=ht: e.tensor_tensor(
                    sq_b[h % 2][:, :], x1[:, h * TH + qc0: h * TH + qc0 + TQ],
                    x1[:, h * TH + qc0: h * TH + qc0 + TQ], OP.mult))
                wait("tensor", sv, n)
                mmn = inc("tensor", smm, lambda e, h=ht, s=(ht == 0), p=(ht == NT - 1):
                          e.matmul(ps[7][0:1, :], ones_sb[:, 0:1], sq_b[h % 2][:, :],
                                   start=s, stop=p))
                sqmm.append(mmn)
            sumsq_mm = sqmm[-1]
            wait_settle("vector", smm, sumsq_mm)
            just("vector", lambda e: e.tensor_scalar(we_row[0:1, 0:TQ], ps[6][0:1, :],
                                                     INV_D, None, OP.mult))
            just("vector", lambda e: e.tensor_tensor(we_row[0:1, TQ:2 * TQ], we_row[0:1, 0:TQ],
                                                     we_row[0:1, 0:TQ], OP.mult))
            just("vector", lambda e: e.scalar_tensor_tensor(
                we_row[0:1, TQ:2 * TQ], ps[7][0:1, :], INV_D, we_row[0:1, TQ:2 * TQ],
                OP.mult, OP.subtract))
            rows_n = inc("vector", sv, lambda e: e.tensor_scalar(
                we_row[0:1, TQ:2 * TQ], we_row[0:1, TQ:2 * TQ], EPS, None, OP.add))
            wait("scalar", sv, rows_n)
            just("scalar", lambda e: e.activation(we_row[0:1, TQ:2 * TQ], we_row[0:1, TQ:2 * TQ],
                                                  AF.Ln))
            inc("scalar", sact, lambda e: e.activation(
                we_row[0:1, TQ:2 * TQ], we_row[0:1, TQ:2 * TQ], AF.Exp, scale=-0.5))
            rstd_n = act_flush()
            wait("tensor", sact, rstd_n)
            wait("tensor", sv, rows_n)
            just("tensor", lambda e: e.matmul(ps[6][:, :], ones_sb[0:1, :],
                                              we_row[0:1, 0:TQ], start=True, stop=True))
            bcast_mm = inc("tensor", smm, lambda e: e.matmul(
                ps[7][:, :], ones_sb[0:1, :], we_row[0:1, TQ:2 * TQ], start=True, stop=True))
            mean_ap, rstd_ap = ps[6], ps[7]
            wait_settle("vector", smm, bcast_mm)
        else:
            xln = premix["xp"]
            wait("vector", smm, premix["vgrp15"])
            mean_ap, rstd_ap = mean_sb, rstd_sb

        mix_n = 0
        mix_ht = [0] * NT
        for ht in range(NT):
            xsl = x1[:, ht * TH + qc0: ht * TH + qc0 + TQ]
            if premixed:
                wait("vector", ssc, xln[ht])
            just("vector", lambda e, h=ht: e.tensor_copy(lnx[:, 0:1], bnd[:, h:h + 1]))
            just("vector", lambda e, s=xsl, m=mean_ap: e.tensor_tensor(
                lnx[:, 1:TQ + 1], s, m[:, :], OP.subtract))
            just("vector", lambda e, r=rstd_ap: e.tensor_tensor(
                lnx[:, 1:TQ + 1], lnx[:, 1:TQ + 1], r[:, :], OP.mult))
            just("vector", lambda e, h=ht: e.tensor_copy(bnd[:, h:h + 1],
                                                         lnx[:, TQ:TQ + 1]))
            just("vector", lambda e: e.tensor_tensor(
                d_b[:, :], lnx[:, 1:TQ + 1], lnx[:, 0:TQ], OP.subtract))
            just("vector", lambda e, h=ht: e.scalar_tensor_tensor(
                xk_v[:, h * TQ:(h + 1) * TQ], d_b[:, :], tmk_c[:, h:h + 1],
                lnx[:, 0:TQ], OP.mult, OP.add))
            just("vector", lambda e, h=ht: e.scalar_tensor_tensor(
                xv_v[:, h * TQ:(h + 1) * TQ], d_b[:, :], tmv_c[:, h:h + 1],
                lnx[:, 0:TQ], OP.mult, OP.add))
            mix_n = inc("vector", sv, lambda e, h=ht: e.scalar_tensor_tensor(
                xr_v[:, h * TQ:(h + 1) * TQ], d_b[:, :], tmr_c[:, h:h + 1],
                lnx[:, 0:TQ], OP.mult, OP.add))
            mix_ht[ht] = mix_n

        if prefetch_next:
            qn = q + 1
            qc1 = (qn % 2) * TQ
            if q == 2:
                wait("scalar", sgd, out_dma_done[0])
            xp = []
            for ht in range(NT):
                n = inc("scalar", ssc,
                        lambda e, h=ht: e.dma_start(
                            out=x1[:, h * TH + qc1: h * TH + qc1 + TQ],
                            in_=xT[h][:, qn * TQ:(qn + 1) * TQ]), 16)
                xp.append(n)
            pm_sqmm = [0] * NT
            pm_sqf = [0] * NT
            pm_state = {"rows": 0, "rstd": 0, "bc": 0, "cp": 0}

        kv_done = [0] * NT
        wkv_done = [0] * NT
        sig_done = [0] * NT
        kcp_done = [0] * NT
        r_grp = [0] * NT
        k_grp = [0] * NT
        v_grp = [0] * NT

        def post_c(c):
            if not prefetch_next:
                return
            if 8 <= c <= 11:
                for t in range(4 * (c - 8), 4 * (c - 8) + 4):
                    if t == 0:
                        wait("tensor", sv, mix_ht[NT - 1])
                    wait("tensor", ssc, xp[t])
                    xsl = x1[:, t * TH + qc1: t * TH + qc1 + TQ]
                    just("tensor", lambda e, s=xsl, st=(t == 0), p=(t == NT - 1):
                         e.matmul(ps[6][0:1, :], ones_sb[:, 0:1], s,
                                  start=st, stop=p))
                    if t >= 2:
                        wait("scalar", smm, pm_sqmm[t - 2])
                    wait("scalar", ssc, xp[t])
                    just("scalar", lambda e, s=xsl, tt=t: e.activation(
                        sq_b[tt % 2][:, :], s, AF.Square))
                    pm_sqf[t] = act_flush()
                    wait("tensor", sact, pm_sqf[t])
                    pm_sqmm[t] = inc("tensor", smm,
                                     lambda e, tt=t, st=(t == 0), p=(t == NT - 1):
                                     e.matmul(ps[7][0:1, :], ones_sb[:, 0:1],
                                              sq_b[tt % 2][:, :], start=st, stop=p))
            elif c == 12:
                wait_settle("vector", smm, pm_sqmm[NT - 1])
                just("vector", lambda e: e.tensor_scalar(
                    rows_sb[0:1, 0:TQ], ps[6][0:1, :], INV_D, None, OP.mult))
                just("vector", lambda e: e.tensor_tensor(
                    rows_sb[0:1, TQ:TH], rows_sb[0:1, 0:TQ], rows_sb[0:1, 0:TQ],
                    OP.mult))
                just("vector", lambda e: e.scalar_tensor_tensor(
                    rows_sb[0:1, TQ:TH], ps[7][0:1, :], INV_D, rows_sb[0:1, TQ:TH],
                    OP.mult, OP.subtract))
                pm_state["rows"] = inc("vector", sv, lambda e: e.tensor_scalar(
                    rows_sb[0:1, TQ:TH], rows_sb[0:1, TQ:TH], EPS, None, OP.add))
                wait("scalar", sv, pm_state["rows"])
                just("scalar", lambda e: e.activation(rows_sb[0:1, TQ:TH],
                                                      rows_sb[0:1, TQ:TH], AF.Ln))
                inc("scalar", sact, lambda e: e.activation(
                    rows_sb[0:1, TQ:TH], rows_sb[0:1, TQ:TH], AF.Exp, scale=-0.5))
                pm_state["rstd"] = act_flush()
            elif c == 13:
                wait("tensor", sact, pm_state["rstd"])
                wait("tensor", sv, pm_state["rows"])
                just("tensor", lambda e: e.matmul(ps[6][:, :], ones_sb[0:1, :],
                                                  rows_sb[0:1, 0:TQ],
                                                  start=True, stop=True))
                pm_state["bc"] = inc("tensor", smm, lambda e: e.matmul(
                    ps[7][:, :], ones_sb[0:1, :], rows_sb[0:1, TQ:TH],
                    start=True, stop=True))
                wait_settle("vector", smm, pm_state["bc"])
                just("vector", lambda e: e.tensor_copy(mean_sb[:, :], ps[6][:, :]))
                pm_state["cp"] = inc("vector", sv, lambda e: e.tensor_copy(
                    rstd_sb[:, :], ps[7][:, :]))

        def prefetch_ot(c):
            if c >= NT:
                return
            s_r, s_k, s_v = c % 2, 2 + c % 2, 4 + c % 2
            if c >= 2:
                slot_free_at[s_r] = r_grp[c - 2]
                slot_free_at[s_k] = k_grp[c - 2]
                slot_free_at[s_v] = v_grp[c - 2]
            load_slab("sync", wr[c], s_r)
            load_slab("gpsimd", wk[c], s_k)
            load_slab("gpsimd", wv[c], s_v)

        prefetch_ot(0)
        prefetch_ot(1)
        for c in range(NT):
            s_r, s_k, s_v = c % 2, 2 + c % 2, 4 + c % 2
            if c >= 2:
                wait("tensor", sact, kcp_done[c - 2])
                wait("tensor", sv, kv_done[c - 2])
            wait_slab("tensor", s_r)
            for kt in range(NT):
                if c < 2:
                    wait("tensor", sv, mix_ht[kt])
                if kt < NT - 1:
                    just("tensor", lambda e, k=kt, sl=s_r, cc=c, s=(kt == 0):
                         e.matmul(ps[0 + cc % 2][:, :],
                                  slab[sl][:, k * 128:(k + 1) * 128],
                                  xr_v[:, k * TQ:(k + 1) * TQ], start=s, stop=False))
                else:
                    r_grp[c] = inc("tensor", smm, lambda e, k=kt, sl=s_r, cc=c:
                                   e.matmul(ps[0 + cc % 2][:, :],
                                            slab[sl][:, k * 128:(k + 1) * 128],
                                            xr_v[:, k * TQ:(k + 1) * TQ],
                                            start=False, stop=True))
            wait_slab("tensor", s_k)
            for kt in range(NT):
                if c < 2:
                    wait("tensor", sv, mix_ht[kt])
                if kt < NT - 1:
                    just("tensor", lambda e, k=kt, sl=s_k, cc=c, s=(kt == 0):
                         e.matmul(ps[2 + cc % 2][:, :],
                                  slab[sl][:, k * 128:(k + 1) * 128],
                                  xk_v[:, k * TQ:(k + 1) * TQ], start=s, stop=False))
                else:
                    k_grp[c] = inc("tensor", smm, lambda e, k=kt, sl=s_k, cc=c:
                                   e.matmul(ps[2 + cc % 2][:, :],
                                            slab[sl][:, k * 128:(k + 1) * 128],
                                            xk_v[:, k * TQ:(k + 1) * TQ],
                                            start=False, stop=True))
            wait_slab("tensor", s_v)
            for kt in range(NT):
                if c < 2:
                    wait("tensor", sv, mix_ht[kt])
                if kt < NT - 1:
                    just("tensor", lambda e, k=kt, sl=s_v, cc=c, s=(kt == 0):
                         e.matmul(ps[4 + cc % 2][:, :],
                                  slab[sl][:, k * 128:(k + 1) * 128],
                                  xv_v[:, k * TQ:(k + 1) * TQ], start=s, stop=False))
                else:
                    v_grp[c] = inc("tensor", smm, lambda e, k=kt, sl=s_v, cc=c:
                                   e.matmul(ps[4 + cc % 2][:, :],
                                            slab[sl][:, k * 128:(k + 1) * 128],
                                            xv_v[:, k * TQ:(k + 1) * TQ],
                                            start=False, stop=True))
            prefetch_ot(c + 2)
            post_c(c)

            wait_settle("scalar", smm, r_grp[c])
            if c >= 2:
                wait("scalar", sv, wkv_done[c - 2])
            inc("scalar", sact, lambda e, cc=c: e.activation(
                r_buf[cc % 2][:, :], ps[0 + cc % 2][:, :], AF.Sigmoid))
            wait_settle("scalar", smm, k_grp[c])
            inc("scalar", sact, lambda e, cc=c: e.activation(
                k_sb[cc % 2][:, :], ps[2 + cc % 2][:, :], AF.Copy))
            sig_done[c] = kcp_done[c] = act_flush()

            wait("vector", sact, kcp_done[c])
            wait_settle("vector", smm, v_grp[c])
            if q == 0 and c == 0:
                wait("vector", sact, wcol_n)
            kv_done[c] = inc("vector", sv, lambda e, cc=c: e.tensor_tensor(
                kv_b[:, :], ps[4 + cc % 2][:, :], k_sb[cc % 2][:, :], OP.mult))
            just("vector", lambda e, cc=c: e.tensor_copy(S_b[:, 0:1],
                                                         state[:, cc:cc + 1]))
            just("vector", lambda e, cc=c: e.tensor_tensor_scan(
                S_b[:, 1:TQ + 1],
                broadcast_tensor_aps(w_c[:, cc:cc + 1], kv_b[:, :])[0],
                kv_b[:, :], S_b[:, 0:1], OP.mult, OP.add))
            just("vector", lambda e, cc=c: e.tensor_copy(state[:, cc:cc + 1],
                                                         S_b[:, TQ:TQ + 1]))
            just("vector", lambda e, cc=c: e.scalar_tensor_tensor(
                pre_b[:, :], kv_b[:, :], tf_c[:, cc:cc + 1], S_b[:, 0:TQ],
                OP.mult, OP.add))
            wait("vector", sact, sig_done[c])
            wkv_done[c] = inc("vector", sv, lambda e, cc=c: e.tensor_tensor(
                wkv_v[:, cc * TQ:(cc + 1) * TQ], pre_b[:, :], r_buf[cc % 2][:, :],
                OP.mult))

        wo_grp = [0] * NT
        wo_ev = [0] * NT

        def prefetch_wo(h):
            if h >= NT:
                return
            sl = h % 2
            if h < 2:
                slot_free_at[sl] = r_grp[NT - 2 + h]
            else:
                slot_free_at[sl] = wo_grp[h - 2]
            load_slab("sync", wo[h], sl)

        prefetch_wo(0)
        prefetch_wo(1)
        tc_h = q % 2
        sqh = [0] * NT
        h2_mm = [0] * NT

        def emit_ln2_stats(j):
            xs = x1[:, j * TH + qc0: j * TH + qc0 + TQ]
            wait("tensor", sv, wo_ev[j])
            just("tensor", lambda e, s=xs, st=(j == 0), t=tc_h: e.matmul(
                ps[4 + t][0:1, :], ones_sb[:, 0:1], s,
                start=st, stop=(j == NT - 1)))
            wait("tensor", sv, sqh[j])
            h2_mm[j] = inc("tensor", smm,
                           lambda e, jj=j, st=(j == 0), t=tc_h: e.matmul(
                               ps[2 + t][0:1, :], ones_sb[:, 0:1],
                               sq_b2[jj % 3][:, :],
                               start=st, stop=(j == NT - 1)))

        for h in range(NT):
            sl = h % 2
            wait_slab("tensor", sl)
            if h == 0:
                wait("tensor", sv, wkv_done[NT - 1])
                wait("tensor", sact, kcp_done[NT - 1])
            if h >= 2:
                wait("tensor", sv, wo_ev[h - 2])
            for ot in range(NT):
                if ot < NT - 1:
                    just("tensor", lambda e, o=ot, s=(ot == 0), hh=h, ss=sl:
                         e.matmul(ps[6 + hh % 2][:, :],
                                  slab[ss][:, o * 128:(o + 1) * 128],
                                  wkv_v[:, o * TQ:(o + 1) * TQ], start=s, stop=False))
                else:
                    wo_grp[h] = inc("tensor", smm, lambda e, o=ot, hh=h, ss=sl:
                                    e.matmul(ps[6 + hh % 2][:, :],
                                             slab[ss][:, o * 128:(o + 1) * 128],
                                             wkv_v[:, o * TQ:(o + 1) * TQ],
                                             start=False, stop=True))
            prefetch_wo(h + 2)
            wait_settle("vector", smm, wo_grp[h])
            xsl = x1[:, h * TH + qc0: h * TH + qc0 + TQ]
            wo_ev[h] = inc("vector", sv, lambda e, s=xsl, hh=h: e.tensor_tensor(
                s, s, ps[6 + hh % 2][:, :], OP.add))
            if h >= 3:
                wait("vector", smm, h2_mm[h - 3])
            sqh[h] = inc("vector", sv, lambda e, s=xsl, hh=h: e.tensor_tensor(
                sq_b2[hh % 3][:, :], s, s, OP.mult))
            if h >= 2:
                emit_ln2_stats(h - 2)
        for hh2 in (NT - 2, NT - 1):
            emit_ln2_stats(hh2)
        wait_settle("vector", smm, h2_mm[NT - 1])
        just("vector", lambda e, t=tc_h: e.tensor_scalar(
            m_row[0:1, t * TQ:(t + 1) * TQ], ps[4 + t][0:1, :], INV_D, None,
            OP.mult))
        stats_ev = inc("vector", sv, lambda e, t=tc_h: e.tensor_scalar(
            s_row[0:1, t * TQ:(t + 1) * TQ], ps[2 + t][0:1, :], INV_D, None,
            OP.mult))
        stats_done[q] = stats_ev

        slot_free_at[0] = wo_grp[NT - 2]
        slot_free_at[1] = wo_grp[NT - 1]
        slot_free_at[2] = k_grp[NT - 2]
        slot_free_at[3] = k_grp[NT - 1]
        slot_free_at[4] = v_grp[NT - 2]
        slot_free_at[5] = v_grp[NT - 1]

        premix.clear()
        if prefetch_next:
            premix.update({"for_q": q + 1, "xp": xp, "vgrp15": v_grp[NT - 1]})

    def emit_ffn(half, dbg_on=False):
        wait("vector", smm, smm.n)
        wait("vector", sact, sact.n)
        wait("tensor", sv, sv.n)
        wait("tensor", sact, sact.n)

        wait("vector", sv, stats_done[2 * half + 1])
        just("vector", lambda e: e.tensor_tensor(we_row[0:1, :], m_row[0:1, :],
                                                 m_row[0:1, :], OP.mult))
        just("vector", lambda e: e.tensor_tensor(s_row[0:1, :], s_row[0:1, :],
                                                 we_row[0:1, :], OP.subtract))
        rows_n = inc("vector", sv, lambda e: e.tensor_scalar(
            s_row[0:1, :], s_row[0:1, :], EPS, None, OP.add))
        wait("scalar", sv, rows_n)
        just("scalar", lambda e: e.activation(s_row[0:1, :], s_row[0:1, :],
                                              AF.Ln))
        inc("scalar", sact, lambda e: e.activation(
            s_row[0:1, :], s_row[0:1, :], AF.Exp, scale=-0.5))
        rstd_n = act_flush()
        wait("tensor", sact, rstd_n)
        wait("tensor", sv, rows_n)
        bcast_mm = 0
        for tc in range(2):
            just("tensor", lambda e, t=tc: e.matmul(
                ps[4 + t][:, :], ones_sb[0:1, :], m_row[0:1, t * TQ:(t + 1) * TQ],
                start=True, stop=True))
            bcast_mm = inc("tensor", smm, lambda e, t=tc: e.matmul(
                ps[6 + t][:, :], ones_sb[0:1, :], s_row[0:1, t * TQ:(t + 1) * TQ],
                start=True, stop=True))
        wait_settle("vector", smm, bcast_mm)
        ln2_n = 0
        ln2_ht = [0] * NT
        for ht in range(NT):
            for tc in range(2):
                xsl = x1[:, ht * TH + tc * TQ: ht * TH + tc * TQ + TQ]
                just("vector", lambda e, s=xsl, t=tc: e.tensor_tensor(
                    gs_b[0][:, :], s, ps[4 + t][:, :], OP.subtract))
                ln2_n = inc("vector", sv, lambda e, h=ht, t=tc: e.tensor_tensor(
                    ln2_v[:, h * TH + t * TQ: h * TH + (t + 1) * TQ],
                    gs_b[0][:, :], ps[6 + t][:, :], OP.mult))
            ln2_ht[ht] = ln2_n

        wrow_n = [0]

        def emit_router():
            wait("tensor", sv, ln2_n)
            wait("tensor", sgd, wrf_n)
            lg_mm = 0
            for tc in range(2):
                for ht in range(NT):
                    if ht < NT - 1:
                        just("tensor", lambda e, h=ht, t=tc, st=(ht == 0):
                             e.matmul(ps[t][0:8, :], wrf_sb[:, h * 8:(h + 1) * 8],
                                      ln2_v[:, h * TH + t * TQ: h * TH + (t + 1) * TQ],
                                      start=st, stop=False))
                    else:
                        lg_mm = inc("tensor", smm, lambda e, h=ht, t=tc:
                                    e.matmul(ps[t][0:8, :], wrf_sb[:, h * 8:(h + 1) * 8],
                                             ln2_v[:, h * TH + t * TQ:
                                                   h * TH + (t + 1) * TQ],
                                             start=False, stop=True))
            wait_settle("vector", smm, lg_mm)
            lgc_n = 0
            for tc in range(2):
                lgc_n = inc("vector", sv, lambda e, t=tc: e.tensor_copy(
                    lg_sb[0:8, t * TQ:(t + 1) * TQ], ps[t][0:8, :]))
            wait("tensor", sv, lgc_n)
            lt_mm = 0
            for tcol in range(8):
                if tcol < 7:
                    just("tensor", lambda e, t=tcol: e.transpose(
                        ps[2][:, t * 8:(t + 1) * 8], lg_sb[0:8, t * 128:(t + 1) * 128],
                        id_sb[0:8, 0:8]))
                else:
                    lt_mm = inc("tensor", smm, lambda e, t=tcol: e.transpose(
                        ps[2][:, t * 8:(t + 1) * 8], lg_sb[0:8, t * 128:(t + 1) * 128],
                        id_sb[0:8, 0:8]))
            wait_settle("vector", smm, lt_mm)
            for tcol in range(8):
                just("vector", lambda e, t=tcol: e.tensor_copy(
                    l8[:, :], ps[2][:, t * 8:(t + 1) * 8]))
                just("vector", lambda e: e.reduce_max(m1_b[:, :], l8[:, :], axis=AX.X))
                just("vector", lambda e: e.tensor_scalar(eq1_b[:, :], l8[:, :],
                                                         m1_b[:, 0:1], None, OP.is_equal))
                just("vector", lambda e: e.scalar_tensor_tensor(
                    w8[:, :], eq1_b[:, :], -1e9, l8[:, :], OP.mult, OP.add))
                just("vector", lambda e: e.reduce_max(m2_b[:, :], w8[:, :], axis=AX.X))
                just("vector", lambda e: e.tensor_scalar(eq2_b[:, :], w8[:, :],
                                                         m2_b[:, 0:1], None, OP.is_equal))
                n = inc("vector", sv, lambda e: e.tensor_tensor(
                    p1_b[:, :], m1_b[:, :], m2_b[:, :], OP.subtract))
                wait("scalar", sv, n)
                n2 = inc("scalar", sact, lambda e: e.activation(p1_b[:, :], p1_b[:, :],
                                                                AF.Sigmoid))
                wait("vector", sact, n2)
                just("vector", lambda e: e.tensor_scalar(p2_b[:, :], p1_b[:, :],
                                                         -1.0, 1.0, OP.mult, OP.add))
                just("vector", lambda e: e.tensor_scalar(t8[:, :], eq1_b[:, :],
                                                         p1_b[:, 0:1], None, OP.mult))
                wn = inc("vector", sv, lambda e: e.scalar_tensor_tensor(
                    w8[:, :], eq2_b[:, :], p2_b[:, 0:1], t8[:, :], OP.mult, OP.add))
                wait("tensor", sv, wn)
                if tcol == 4:
                    wait("tensor", sv, wrow_n[0])
                tn = inc("tensor", smm, lambda e, t=tcol: e.transpose(
                    ps[3][0:8, (t % 4) * 128:(t % 4 + 1) * 128], w8[:, :], id_sb[:, :]))
                if tcol == 3 or tcol == 7:
                    wait_settle("vector", smm, tn)
                    base = 0 if tcol == 3 else 512
                    wrow_n[0] = inc("vector", sv, lambda e, b=base: e.tensor_copy(
                        wr_sb[0:8, b:b + 512], ps[3][0:8, :]))

            if dbg_on:
                wait("gpsimd", sv, sv.n)
                inc("gpsimd", sgd, lambda e: e.dma_start(out=dbg_ln2, in_=bigA[:, 0:NT * TH]), 16)
                inc("gpsimd", sgd, lambda e: e.dma_start(out=dbg_wr, in_=wr_sb), 16)
                inc("gpsimd", sgd, lambda e: e.dma_start(out=dbg_lg, in_=lg_sb), 16)
                inc("gpsimd", sgd, lambda e: e.dma_start(out=dbg_sm, in_=arena[:, 5664:5760]), 16)
                holdr = sgd.n
                wait("tensor", sgd, holdr)


        g_grp = {}
        u_grp = {}
        d_grp = {}
        z_done = {}
        d_ev = {}
        sil_done = {}
        we_bufs = [(we_row, we_b), (we_row2, we_b2)]
        we_mm_par = [0, 0]
        we_cp_e = {}
        last_dev_for_ht = [0] * NT

        def pf_gu(e_, it):
            if it >= NT:
                e_, it = e_ + 1, it - NT
            if e_ >= NE:
                return
            sg, su = it % 2, 2 + it % 2
            if it >= 2:
                slot_free_at[sg] = g_grp[(e_, it - 2, 1)]
                slot_free_at[su] = u_grp[(e_, it - 2, 1)]
            elif e_ > 0:
                slot_free_at[sg] = g_grp[(e_ - 1, NT - 2 + it, 1)]
                slot_free_at[su] = u_grp[(e_ - 1, NT - 2 + it, 1)]
            load_slab("sync", wg[e_, it], sg)
            load_slab("sync", wu[e_, it], su)

        def pf_d(e_, ht):
            if ht >= NT:
                e_, ht = e_ + 1, ht - NT
            if e_ >= NE:
                return
            sd = 4 + ht % 2
            if ht >= 2:
                slot_free_at[sd] = d_grp[(e_, ht - 2, 1)]
            elif e_ > 0:
                slot_free_at[sd] = d_grp[(e_ - 1, NT - 2 + ht, 1)]
            load_slab("gpsimd", wd[e_, ht], sd)

        def emit_unit(e_, it, tc, early=False):
            sg, su = it % 2, 2 + it % 2
            unit = it * 2 + tc
            pg, pu = ps[unit % 2], ps[2 + unit % 2]
            if e_ == 0 and it < 2:
                wait("tensor", sv, rows_n)
            if e_ > 0 and unit <= 1:
                wait("tensor", sv, we_cp_e[e_])
            if it >= 1:
                wait("tensor", sv, z_done[(e_, it - 1, tc)])
            elif e_ > 0:
                wait("tensor", sv, z_done[(e_ - 1, NT - 1, tc)])
            wait_slab("tensor", sg)
            for kt in range(NT):
                if early:
                    wait("tensor", sv, ln2_ht[kt])
                if kt < NT - 1:
                    just("tensor",
                         lambda e, k=kt, s=(kt == 0), b=pg, ss=sg, t=tc:
                         e.matmul(b[:, :],
                                  slab[ss][:, k * 128:(k + 1) * 128],
                                  ln2_v[:, k * TH + t * TQ:
                                        k * TH + (t + 1) * TQ],
                                  start=s, stop=False))
                else:
                    g_grp[(e_, it, tc)] = inc(
                        "tensor", smm,
                        lambda e, k=kt, b=pg, ss=sg, t=tc:
                        e.matmul(b[:, :],
                                 slab[ss][:, k * 128:(k + 1) * 128],
                                 ln2_v[:, k * TH + t * TQ:
                                       k * TH + (t + 1) * TQ],
                                 start=False, stop=True))
            wait_slab("tensor", su)
            for kt in range(NT):
                if early:
                    wait("tensor", sv, ln2_ht[kt])
                if kt < NT - 1:
                    just("tensor",
                         lambda e, k=kt, s=(kt == 0), b=pu, ss=su, t=tc:
                         e.matmul(b[:, :],
                                  slab[ss][:, k * 128:(k + 1) * 128],
                                  ln2_v[:, k * TH + t * TQ:
                                        k * TH + (t + 1) * TQ],
                                  start=s, stop=False))
                else:
                    u_grp[(e_, it, tc)] = inc(
                        "tensor", smm,
                        lambda e, k=kt, b=pu, ss=su, t=tc:
                        e.matmul(b[:, :],
                                 slab[ss][:, k * 128:(k + 1) * 128],
                                 ln2_v[:, k * TH + t * TQ:
                                       k * TH + (t + 1) * TQ],
                                 start=False, stop=True))
            wait_settle("scalar", smm, g_grp[(e_, it, tc)])
            if it >= 1:
                wait("scalar", sv, z_done[(e_, it - 1, tc)])
            elif e_ > 0:
                wait("scalar", sv, z_done[(e_ - 1, NT - 1, tc)])
            inc("scalar", sact, lambda e, u=unit: e.activation(
                gs_b[u % 2][:, :], ps[u % 2][:, :], AF.Silu))
            sil_done[(e_, it, tc)] = act_flush()
            wait("vector", sact, sil_done[(e_, it, tc)])
            wait_settle("vector", smm, u_grp[(e_, it, tc)])
            just("vector", lambda e, u=unit: e.tensor_scalar(
                gs_b[u % 2][:, :], gs_b[u % 2][:, :], -CLIP, CLIP,
                OP.max, OP.min))
            z_done[(e_, it, tc)] = inc("vector", sv,
                                       lambda e, u=unit, b=pu, h=it, t=tc:
                                       e.tensor_tensor(
                z_v[:, h * TH + t * TQ: h * TH + (t + 1) * TQ],
                gs_b[u % 2][:, :], b[:, :], OP.mult))

        pf_gu(0, 0)
        pf_gu(0, 1)
        pf_d(0, 0)
        pf_d(0, 1)

        for it0 in range(NT):
            for tc0 in (0, 1):
                emit_unit(0, it0, tc0, early=(it0 < 4))
            pf_gu(0, it0 + 2)
        emit_router()

        def emit_we_dma(e_):
            wrx = we_bufs[e_ % 2][0]
            wait_settle("scalar", sv, wrow_n[0])
            wait("scalar", smm, we_mm_par[e_ % 2])
            return inc("scalar", ssc, lambda e, x=e_, w=wrx: e.dma_start(
                out=w[0:1, :], in_=wr_sb[x:x + 1, :]), 16)

        def emit_we_bcast(e_, wrdma, pa, pb):
            wrx, wbx = we_bufs[e_ % 2]
            wait("tensor", ssc, wrdma)
            just("tensor", lambda e, p=pa, w=wrx: e.matmul(
                p[:, :], ones_sb[0:1, :], w[0:1, 0:TQ], start=True, stop=True))
            we_mm_par[e_ % 2] = inc("tensor", smm, lambda e, p=pb, w=wrx:
                                    e.matmul(p[:, :], ones_sb[0:1, :],
                                             w[0:1, TQ:TH],
                                             start=True, stop=True))
            wait_settle("vector", smm, we_mm_par[e_ % 2])
            just("vector", lambda e, p=pa, w=wbx: e.tensor_copy(w[:, 0:TQ],
                                                                p[:, :]))
            we_cp_e[e_] = inc("vector", sv, lambda e, p=pb, w=wbx:
                              e.tensor_copy(w[:, TQ:TH], p[:, :]))

        pend_wrdma = [0]
        for e_ in range(NE):
            if e_ == 0:
                emit_we_bcast(0, emit_we_dma(0), ps[6], ps[7])
            we_cur = we_bufs[e_ % 2][1]
            for it in range(0 if e_ > 0 else NT, NT):
                for tc in range(2):
                    emit_unit(e_, it, tc)
                pf_gu(e_, it + 2)
            for ht in range(NT):
                sd = 4 + ht % 2
                wait_slab("tensor", sd)
                for tc in range(2):
                    gi = ht * 2 + tc
                    pd = ps[4 + gi % 4]
                    if gi >= 4:
                        wait("tensor", sv, d_ev[(e_, (gi - 4) // 2, (gi - 4) % 2)])
                    elif e_ > 0:
                        wait("tensor", sv,
                             d_ev[(e_ - 1, (28 + gi) // 2, (28 + gi) % 2)])
                    else:
                        wait("tensor", sv, we_cp_e[0])
                    if gi == 0:
                        wait("tensor", sv, z_done[(e_, NT - 1, 1)])
                    for kt in range(NT):
                        if kt < NT - 1:
                            just("tensor",
                                 lambda e, k=kt, s=(kt == 0), b=pd, ss=sd, t=tc:
                                 e.matmul(b[:, :],
                                          slab[ss][:, k * 128:(k + 1) * 128],
                                          z_v[:, k * TH + t * TQ:
                                              k * TH + (t + 1) * TQ],
                                          start=s, stop=False))
                        else:
                            d_grp[(e_, ht, tc)] = inc(
                                "tensor", smm,
                                lambda e, k=kt, b=pd, ss=sd, t=tc:
                                e.matmul(b[:, :],
                                         slab[ss][:, k * 128:(k + 1) * 128],
                                         z_v[:, k * TH + t * TQ:
                                             k * TH + (t + 1) * TQ],
                                         start=False, stop=True))
                    wait_settle("vector", smm, d_grp[(e_, ht, tc)])
                    xsl = x1[:, ht * TH + tc * TQ: ht * TH + tc * TQ + TQ]
                    just("vector", lambda e, b=pd, t=tc, w=we_cur:
                         e.tensor_tensor(d_b[:, :], b[:, :],
                                         w[:, t * TQ:(t + 1) * TQ], OP.mult))
                    d_ev[(e_, ht, tc)] = inc(
                        "vector", sv, lambda e, s=xsl: e.tensor_tensor(
                            s, s, d_b[:, :], OP.add))
                    if e_ == NE - 1 and tc == 1:
                        last_dev_for_ht[ht] = d_ev[(e_, ht, tc)]
                pf_d(e_, ht + 2)
                if e_ < NE - 1:
                    if ht == 0:
                        pend_wrdma[0] = emit_we_dma(e_ + 1)
                    elif ht == 2:
                        emit_we_bcast(e_ + 1, pend_wrdma[0], ps[2], ps[3])

        eL = NE - 1
        slot_free_at[0] = g_grp[(eL, NT - 2, 1)]
        slot_free_at[1] = g_grp[(eL, NT - 1, 1)]
        slot_free_at[2] = u_grp[(eL, NT - 2, 1)]
        slot_free_at[3] = u_grp[(eL, NT - 1, 1)]
        slot_free_at[4] = d_grp[(eL, NT - 2, 1)]
        slot_free_at[5] = d_grp[(eL, NT - 1, 1)]

        for ht in range(NT):
            wait_settle("gpsimd", sv, last_dev_for_ht[ht])
            inc("gpsimd", sgd, lambda e, h=ht: e.dma_start(
                out=y[half, h], in_=x1[:, h * TH:(h + 1) * TH]), 16)
        out_dma_done[half] = sgd.n

    emit_quarter(0, prefetch_next=True)
    if debug:
        wait("gpsimd", sv, sv.n)
        inc("gpsimd", sgd, lambda e: e.dma_start(out=dbg_xk, in_=bigA[:, 0:NT * TQ * 3]), 16)
        inc("gpsimd", sgd, lambda e: e.dma_start(out=dbg_wkv, in_=bigA[:, 3 * NT * TQ:4 * NT * TQ]), 16)
        hold0 = sgd.n
        wait("tensor", sgd, hold0)
    emit_quarter(1)
    if debug:
        wait("gpsimd", sv, sv.n)
        inc("gpsimd", sgd, lambda e: e.dma_start(out=dbg_x1, in_=x1[:, :]), 16)
        hold1 = sgd.n
        wait("tensor", sgd, hold1)
    emit_ffn(0, debug)
    emit_quarter(2, prefetch_next=True)
    emit_quarter(3)
    emit_ffn(1, False)
    wait("gpsimd", sgd, sgd.n)

    with ExitStack() as es2:
        for c in ctrs:
            c.sem = es2.enter_context(nc.semaphore(c.name))
        block = es2.enter_context(nc.Block())

        @block.sync
        def _(sync):
            for f in plan["sync"]:
                f(sync)

        @block.tensor
        def _(t):
            for f in plan["tensor"]:
                f(t)

        @block.vector
        def _(v):
            for f in plan["vector"]:
                f(v)

        @block.scalar
        def _(s):
            for f in plan["scalar"]:
                f(s)

        @block.gpsimd
        def _(g):
            for f in plan["gpsimd"]:
                f(g)

    es.close()
    return nc



def _pretile(weff):
    K, M = weff.shape
    w4 = weff.reshape(K // 128, 128, M // 128, 128)
    return np.ascontiguousarray(w4.transpose(2, 1, 0, 3)).reshape(
        M // 128, 128, (K // 128) * 128)


def _col16(v):
    return np.ascontiguousarray(v.reshape(16, 128).T)


def kernel(L0_attn_hidden_states, L0_attn_Wr, L0_attn_Wk, L0_attn_Wv, L0_attn_Wo,
           L0_attn_time_decay, L0_attn_time_first, L0_attn_time_mix_k,
           L0_attn_time_mix_v, L0_attn_time_mix_r, L0_ffn_Wr, L0_ffn_Wg,
           L0_ffn_Wu, L0_ffn_Wd):
    x = np.asarray(L0_attn_hidden_states, dtype=np.float32)
    B = x.shape[0]
    f32 = lambda a: np.asarray(a, dtype=np.float32)

    bf16 = lambda a: np.ascontiguousarray(a.astype(ml_dtypes.bfloat16))
    shared = {
        "wr": bf16(_pretile(f32(L0_attn_Wr).T)),
        "wk": bf16(_pretile(f32(L0_attn_Wk).T)),
        "wv": bf16(_pretile(f32(L0_attn_Wv).T)),
        "wo": bf16(_pretile(f32(L0_attn_Wo).T)),
        "wg": bf16(np.stack([_pretile(f32(L0_ffn_Wg)[e]) for e in range(NE)])),
        "wu": bf16(np.stack([_pretile(f32(L0_ffn_Wu)[e]) for e in range(NE)])),
        "wd": bf16(np.stack([_pretile(f32(L0_ffn_Wd)[e]) for e in range(NE)])),
        "wrf": bf16(np.ascontiguousarray(
            f32(L0_ffn_Wr).reshape(16, 128, 8).transpose(1, 0, 2).reshape(128, 128))),
        "tmk": _col16(f32(L0_attn_time_mix_k).reshape(-1)),
        "tmv": _col16(f32(L0_attn_time_mix_v).reshape(-1)),
        "tmr": _col16(f32(L0_attn_time_mix_r).reshape(-1)),
        "td": _col16(f32(L0_attn_time_decay).reshape(-1)),
        "tf": _col16(f32(L0_attn_time_first).reshape(-1)),
        "ident": np.eye(128, dtype=np.float32),
        "onesd": np.ones((128, 128), dtype=np.float32),
    }

    nc = build_nc()
    in_maps = []
    for b in range(B):
        m = dict(shared)
        m["xT"] = np.ascontiguousarray(x[b].T).reshape(NT, 128, 2048)
        in_maps.append(m)

    res = run_bass_kernel_spmd(nc, in_maps, core_ids=list(range(N_CORES)))

    out = np.empty((B, 2048, 2048), dtype=np.float32)
    for b in range(B):
        arr = res.results[b]["y"]
        for half in range(2):
            out[b, half * TH:(half + 1) * TH, :] = arr[half].reshape(2048, TH).T
    return out


# version 13
# speedup vs baseline: 1.0785x; 1.0785x over previous
import numpy as np
import ml_dtypes
import concourse.bass as bass
import concourse.mybir as mybir
from concourse.bass_utils import run_bass_kernel_spmd
from contextlib import ExitStack
from concourse.bass import broadcast_tensor_aps

F32 = mybir.dt.float32
BF16 = mybir.dt.bfloat16
OP = mybir.AluOpType
AF = mybir.ActivationFunctionType
AX = mybir.AxisListType

N_CORES = 8
D = 2048
NT = 16
TQ = 512
TH = 1024
NE = 8
EPS = 1e-5
CLIP = 7.0
INV_D = 1.0 / D


class Ctr:
    def __init__(self, name):
        self.name = name
        self.n = 0
        self.sem = None


def build_nc(debug=False):
    nc = bass.Bass("TRN2", target_bir_lowering=False)

    xT = nc.dram_tensor("xT", [NT, 128, 2048], F32, kind="ExternalInput").ap()
    wr = nc.dram_tensor("wr", [NT, 128, 2048], BF16, kind="ExternalInput").ap()
    wk = nc.dram_tensor("wk", [NT, 128, 2048], BF16, kind="ExternalInput").ap()
    wv = nc.dram_tensor("wv", [NT, 128, 2048], BF16, kind="ExternalInput").ap()
    wo = nc.dram_tensor("wo", [NT, 128, 2048], BF16, kind="ExternalInput").ap()
    wg = nc.dram_tensor("wg", [NE, NT, 128, 2048], BF16, kind="ExternalInput").ap()
    wu = nc.dram_tensor("wu", [NE, NT, 128, 2048], BF16, kind="ExternalInput").ap()
    wd = nc.dram_tensor("wd", [NE, NT, 128, 2048], BF16, kind="ExternalInput").ap()
    wrf = nc.dram_tensor("wrf", [128, 128], BF16, kind="ExternalInput").ap()
    tmk = nc.dram_tensor("tmk", [128, 16], F32, kind="ExternalInput").ap()
    tmv = nc.dram_tensor("tmv", [128, 16], F32, kind="ExternalInput").ap()
    tmr = nc.dram_tensor("tmr", [128, 16], F32, kind="ExternalInput").ap()
    td = nc.dram_tensor("td", [128, 16], F32, kind="ExternalInput").ap()
    tf = nc.dram_tensor("tf", [128, 16], F32, kind="ExternalInput").ap()
    ident = nc.dram_tensor("ident", [128, 128], F32, kind="ExternalInput").ap()
    onesd = nc.dram_tensor("onesd", [128, 128], F32, kind="ExternalInput").ap()
    y = nc.dram_tensor("y", [2, NT, 128, TH], F32, kind="ExternalOutput").ap()
    if debug:
        dbg_xk = nc.dram_tensor("dbg_xk", [128, NT * TQ * 3], BF16, kind="ExternalOutput").ap()
        dbg_wkv = nc.dram_tensor("dbg_wkv", [128, NT * TQ], BF16, kind="ExternalOutput").ap()
        dbg_x1 = nc.dram_tensor("dbg_x1", [128, NT * TH], F32, kind="ExternalOutput").ap()
        dbg_ln2 = nc.dram_tensor("dbg_ln2", [128, NT * TH], BF16, kind="ExternalOutput").ap()
        dbg_wr = nc.dram_tensor("dbg_wr", [8, TH], F32, kind="ExternalOutput").ap()
        dbg_lg = nc.dram_tensor("dbg_lg", [8, TH], F32, kind="ExternalOutput").ap()
        dbg_sm = nc.dram_tensor("dbg_sm", [128, 96], F32, kind="ExternalOutput").ap()
        dbg_z = nc.dram_tensor("dbg_z", [128, NT * TH], BF16, kind="ExternalOutput").ap()

    es = ExitStack()
    sb = lambda name, shape, dt=F32: es.enter_context(nc.sbuf_tensor(name, shape, dt))

    x1 = sb("x1", [128, NT * TH])
    bigA = sb("bigA", [128, 2 * NT * TH], BF16)
    xk_v = bigA[:, 0 * NT * TQ: 1 * NT * TQ]
    xv_v = bigA[:, 1 * NT * TQ: 2 * NT * TQ]
    xr_v = bigA[:, 2 * NT * TQ: 3 * NT * TQ]
    wkv_v = bigA[:, 3 * NT * TQ: 4 * NT * TQ]
    ln2_v = bigA[:, 0: NT * TH]
    z_v = bigA[:, NT * TH: 2 * NT * TH]

    slab = [sb(f"slab{i}", [128, 2048], BF16) for i in range(6)]

    arena = sb("arena", [128, 5760])
    r_buf = [arena[:, 0:512], arena[:, 512:1024]]
    k_sb = [arena[:, 1024:1536], arena[:, 1536:2048]]
    kv_b = arena[:, 2048:2560]
    S_b = arena[:, 2560:3073]
    pre_b = arena[:, 3088:3600]
    lnx = arena[:, 3600:4113]
    d_b = arena[:, 4128:4640]
    wtil = arena[:, 4640:5152]
    lg_sb = arena[0:8, 0:1024]
    wr_sb = arena[0:8, 1024:2048]
    we_row = arena[0:1, 2048:3072]
    we_b = arena[:, 3072:4096]
    gs_b = [arena[:, 4640:5152], arena[:, 5152:5664]]
    w8 = arena[:, 5664:5672]
    l8 = arena[:, 5672:5680]
    eq1_b = arena[:, 5680:5688]
    eq2_b = arena[:, 5688:5696]
    t8 = arena[:, 5696:5704]
    m1_b = arena[:, 5704:5705]
    m2_b = arena[:, 5712:5713]
    p1_b = arena[:, 5720:5721]
    p2_b = arena[:, 5728:5729]
    m_row = sb("m_row", [1, TH])
    s_row = sb("s_row", [1, TH])
    sq_b = [sb(f"sq_b{i}", [128, TQ]) for i in range(2)]
    sq_b2 = [sq_b[0], sq_b[1], sb("sq_b2x", [128, TQ])]
    rows_sb = sb("rows_sb", [1, TH])
    mean_sb = sb("mean_sb", [128, TQ])
    rstd_sb = sb("rstd_sb", [128, TQ])
    we_row2 = sb("we_row2", [1, TH])
    we_b2 = sb("we_b2", [128, TH])

    ones_sb = sb("ones_sb", [128, 128])
    id_sb = sb("id_sb", [128, 128])
    wrf_sb = sb("wrf_sb", [128, 128], BF16)
    tmk_c = sb("tmk_c", [128, 16])
    tmv_c = sb("tmv_c", [128, 16])
    tmr_c = sb("tmr_c", [128, 16])
    td_c = sb("td_c", [128, 16])
    tf_c = sb("tf_c", [128, 16])
    w_c = sb("w_c", [128, 16])
    state = sb("state", [128, 16])
    bnd = sb("bnd", [128, 16])
    flc = sb("flc", [1, 8])

    ps = [es.enter_context(nc.psum_tensor(f"ps{i}", [128, 512], F32)) for i in range(8)]

    ssy = Ctr("ssy")
    sgd = Ctr("sgd")
    sgp = Ctr("sgp")
    smm = Ctr("smm")
    sv = Ctr("sv")
    sact = Ctr("sact")
    ssc = Ctr("ssc")
    ctrs = [ssy, sgd, sgp, smm, sv, sact, ssc]

    plan = {"sync": [], "tensor": [], "vector": [], "scalar": [], "gpsimd": []}
    _self_init = True

    def em(eng, f):
        plan[eng].append(f)

    def wait(eng, ctr, n):
        if n <= 0:
            return
        em(eng, lambda e, c=ctr, k=n: e.wait_ge(c.sem, k))

    SETTLE = 384

    def wait_settle(eng, ctr, n):
        if n <= 0:
            return
        em(eng, lambda e, c=ctr, k=n: e.wait_ge(c.sem, k))
        em(eng, lambda e: e.nop(cycle_cnt=SETTLE))

    _self = {"vector": sv, "scalar": sact, "gpsimd": sgp}

    def inc(eng, ctr, fmake, k=1):
        sc = _self.get(eng)
        if sc is not None and sc.n > 0 and ctr is sc:
            em(eng, lambda e, c=sc, kk=sc.n: e.wait_ge(c.sem, kk))
        ctr.n += k
        em(eng, lambda e, c=ctr, kk=k, fm=fmake: fm(e).then_inc(c.sem, kk))
        return ctr.n

    def just(eng, fmake):
        if eng == "vector":
            return inc("vector", sv, fmake)
        if eng == "scalar":
            return inc("scalar", sact, fmake)
        em(eng, lambda e, fm=fmake: fm(e))

    def act_flush():
        return inc("scalar", sact, lambda e: e.activation(flc[0:1, 0:1],
                                                          flc[0:1, 0:1], AF.Copy))

    slot_free_at = [0] * 6
    slot_dma = [(ssy, 0)] * 6

    def load_slab(queue, dram_slab, slot):
        qc = ssy if queue == "sync" else sgd
        wait(queue, smm, slot_free_at[slot])
        n = inc(queue, qc,
                lambda e, d=dram_slab, s=slot:
                e.dma_start(out=slab[s][:, :], in_=d), 16)
        slot_dma[slot] = (qc, n)

    def wait_slab(eng, slot):
        qc, n = slot_dma[slot]
        wait(eng, qc, n)

    for src, dst in [(ident, id_sb), (onesd, ones_sb),
                     (tmk, tmk_c), (tmv, tmv_c), (tmr, tmr_c),
                     (td, td_c), (tf, tf_c)]:
        inc("sync", ssy, lambda e, s=src, dd=dst: e.dma_start(out=dd[:, :], in_=s), 16)
    consts_n = ssy.n
    wrf_n = inc("gpsimd", sgd, lambda e: e.dma_start(out=wrf_sb[:, :], in_=wrf), 16)
    inc("gpsimd", sgp, lambda e: e.memset(state[:, :], 0.0))
    memset_n = inc("gpsimd", sgp, lambda e: e.memset(bnd[:, :], 0.0))
    wait("scalar", ssy, consts_n)
    just("scalar", lambda e: e.activation(w_c[:, :], td_c[:, :], AF.Exp))
    inc("scalar", sact, lambda e: e.activation(
        w_c[:, :], w_c[:, :], AF.Exp, scale=-1.0))
    wcol_n = act_flush()

    out_dma_done = [0, 0]
    stats_done = [0, 0, 0, 0]
    premix = {}

    def emit_quarter(q, prefetch_next=False):
        qc0 = (q % 2) * TQ
        premixed = premix.get("for_q") == q

        if not premixed:
            if q == 0:
                wait("sync", ssy, consts_n)
            if q == 2:
                wait("sync", sgd, out_dma_done[0])
            xln = []
            for ht in range(NT):
                n = inc("sync", ssy,
                        lambda e, h=ht: e.dma_start(
                            out=x1[:, h * TH + qc0: h * TH + qc0 + TQ],
                            in_=xT[h][:, q * TQ:(q + 1) * TQ]), 16)
                xln.append(n)

            wait("vector", smm, smm.n)
            wait("vector", sact, sact.n)
            wait("tensor", sv, sv.n)
            wait("tensor", sact, sact.n)
            if q == 0:
                wait("vector", sgp, memset_n)

            for ht in range(NT):
                wait("tensor", ssy, xln[ht])
                if ht < NT - 1:
                    just("tensor", lambda e, h=ht, s=(ht == 0):
                         e.matmul(ps[6][0:1, :], ones_sb[:, 0:1],
                                  x1[:, h * TH + qc0: h * TH + qc0 + TQ],
                                  start=s, stop=False))
                else:
                    inc("tensor", smm, lambda e, h=ht:
                        e.matmul(ps[6][0:1, :], ones_sb[:, 0:1],
                                 x1[:, h * TH + qc0: h * TH + qc0 + TQ],
                                 start=False, stop=True))
            sqmm = []
            for ht in range(NT):
                wait("vector", ssy, xln[ht])
                if ht >= 2:
                    wait("vector", smm, sqmm[ht - 2])
                n = inc("vector", sv, lambda e, h=ht: e.tensor_tensor(
                    sq_b[h % 2][:, :], x1[:, h * TH + qc0: h * TH + qc0 + TQ],
                    x1[:, h * TH + qc0: h * TH + qc0 + TQ], OP.mult))
                wait("tensor", sv, n)
                mmn = inc("tensor", smm, lambda e, h=ht, s=(ht == 0), p=(ht == NT - 1):
                          e.matmul(ps[7][0:1, :], ones_sb[:, 0:1], sq_b[h % 2][:, :],
                                   start=s, stop=p))
                sqmm.append(mmn)
            sumsq_mm = sqmm[-1]
            wait_settle("vector", smm, sumsq_mm)
            just("vector", lambda e: e.tensor_scalar(we_row[0:1, 0:TQ], ps[6][0:1, :],
                                                     INV_D, None, OP.mult))
            just("vector", lambda e: e.tensor_tensor(we_row[0:1, TQ:2 * TQ], we_row[0:1, 0:TQ],
                                                     we_row[0:1, 0:TQ], OP.mult))
            just("vector", lambda e: e.scalar_tensor_tensor(
                we_row[0:1, TQ:2 * TQ], ps[7][0:1, :], INV_D, we_row[0:1, TQ:2 * TQ],
                OP.mult, OP.subtract))
            rows_n = inc("vector", sv, lambda e: e.tensor_scalar(
                we_row[0:1, TQ:2 * TQ], we_row[0:1, TQ:2 * TQ], EPS, None, OP.add))
            wait("scalar", sv, rows_n)
            just("scalar", lambda e: e.activation(we_row[0:1, TQ:2 * TQ], we_row[0:1, TQ:2 * TQ],
                                                  AF.Ln))
            inc("scalar", sact, lambda e: e.activation(
                we_row[0:1, TQ:2 * TQ], we_row[0:1, TQ:2 * TQ], AF.Exp, scale=-0.5))
            rstd_n = act_flush()
            wait("tensor", sact, rstd_n)
            wait("tensor", sv, rows_n)
            just("tensor", lambda e: e.matmul(ps[6][:, :], ones_sb[0:1, :],
                                              we_row[0:1, 0:TQ], start=True, stop=True))
            bcast_mm = inc("tensor", smm, lambda e: e.matmul(
                ps[7][:, :], ones_sb[0:1, :], we_row[0:1, TQ:2 * TQ], start=True, stop=True))
            mean_ap, rstd_ap = ps[6], ps[7]
            wait_settle("vector", smm, bcast_mm)
        else:
            xln = premix["xp"]
            wait("vector", smm, premix["vgrp15"])
            mean_ap, rstd_ap = mean_sb, rstd_sb

        mix_n = 0
        mix_ht = [0] * NT
        for ht in range(NT):
            xsl = x1[:, ht * TH + qc0: ht * TH + qc0 + TQ]
            if premixed:
                wait("vector", ssc, xln[ht])
            just("vector", lambda e, h=ht: e.tensor_copy(lnx[:, 0:1], bnd[:, h:h + 1]))
            just("vector", lambda e, s=xsl, m=mean_ap: e.tensor_tensor(
                lnx[:, 1:TQ + 1], s, m[:, :], OP.subtract))
            just("vector", lambda e, r=rstd_ap: e.tensor_tensor(
                lnx[:, 1:TQ + 1], lnx[:, 1:TQ + 1], r[:, :], OP.mult))
            just("vector", lambda e, h=ht: e.tensor_copy(bnd[:, h:h + 1],
                                                         lnx[:, TQ:TQ + 1]))
            just("vector", lambda e: e.tensor_tensor(
                d_b[:, :], lnx[:, 1:TQ + 1], lnx[:, 0:TQ], OP.subtract))
            just("vector", lambda e, h=ht: e.scalar_tensor_tensor(
                xk_v[:, h * TQ:(h + 1) * TQ], d_b[:, :], tmk_c[:, h:h + 1],
                lnx[:, 0:TQ], OP.mult, OP.add))
            just("vector", lambda e, h=ht: e.scalar_tensor_tensor(
                xv_v[:, h * TQ:(h + 1) * TQ], d_b[:, :], tmv_c[:, h:h + 1],
                lnx[:, 0:TQ], OP.mult, OP.add))
            mix_n = inc("vector", sv, lambda e, h=ht: e.scalar_tensor_tensor(
                xr_v[:, h * TQ:(h + 1) * TQ], d_b[:, :], tmr_c[:, h:h + 1],
                lnx[:, 0:TQ], OP.mult, OP.add))
            mix_ht[ht] = mix_n

        if prefetch_next:
            qn = q + 1
            qc1 = (qn % 2) * TQ
            if q == 2:
                wait("scalar", sgd, out_dma_done[0])
            xp = []
            for ht in range(NT):
                n = inc("scalar", ssc,
                        lambda e, h=ht: e.dma_start(
                            out=x1[:, h * TH + qc1: h * TH + qc1 + TQ],
                            in_=xT[h][:, qn * TQ:(qn + 1) * TQ]), 16)
                xp.append(n)
            pm_sqmm = [0] * NT
            pm_sqf = [0] * NT
            pm_state = {"rows": 0, "rstd": 0, "bc": 0, "cp": 0}

        kv_done = [0] * NT
        wkv_done = [0] * NT
        sig_done = [0] * NT
        kcp_done = [0] * NT
        r_grp = [0] * NT
        k_grp = [0] * NT
        v_grp = [0] * NT

        def post_c(c):
            if not prefetch_next:
                return
            if 8 <= c <= 11:
                for t in range(4 * (c - 8), 4 * (c - 8) + 4):
                    if t == 0:
                        wait("tensor", sv, mix_ht[NT - 1])
                    wait("tensor", ssc, xp[t])
                    xsl = x1[:, t * TH + qc1: t * TH + qc1 + TQ]
                    just("tensor", lambda e, s=xsl, st=(t == 0), p=(t == NT - 1):
                         e.matmul(ps[6][0:1, :], ones_sb[:, 0:1], s,
                                  start=st, stop=p))
                    if t >= 2:
                        wait("scalar", smm, pm_sqmm[t - 2])
                    wait("scalar", ssc, xp[t])
                    just("scalar", lambda e, s=xsl, tt=t: e.activation(
                        sq_b[tt % 2][:, :], s, AF.Square))
                    pm_sqf[t] = act_flush()
                    wait("tensor", sact, pm_sqf[t])
                    pm_sqmm[t] = inc("tensor", smm,
                                     lambda e, tt=t, st=(t == 0), p=(t == NT - 1):
                                     e.matmul(ps[7][0:1, :], ones_sb[:, 0:1],
                                              sq_b[tt % 2][:, :], start=st, stop=p))
            elif c == 12:
                wait_settle("vector", smm, pm_sqmm[NT - 1])
                just("vector", lambda e: e.tensor_scalar(
                    rows_sb[0:1, 0:TQ], ps[6][0:1, :], INV_D, None, OP.mult))
                just("vector", lambda e: e.tensor_tensor(
                    rows_sb[0:1, TQ:TH], rows_sb[0:1, 0:TQ], rows_sb[0:1, 0:TQ],
                    OP.mult))
                just("vector", lambda e: e.scalar_tensor_tensor(
                    rows_sb[0:1, TQ:TH], ps[7][0:1, :], INV_D, rows_sb[0:1, TQ:TH],
                    OP.mult, OP.subtract))
                pm_state["rows"] = inc("vector", sv, lambda e: e.tensor_scalar(
                    rows_sb[0:1, TQ:TH], rows_sb[0:1, TQ:TH], EPS, None, OP.add))
                wait("scalar", sv, pm_state["rows"])
                just("scalar", lambda e: e.activation(rows_sb[0:1, TQ:TH],
                                                      rows_sb[0:1, TQ:TH], AF.Ln))
                inc("scalar", sact, lambda e: e.activation(
                    rows_sb[0:1, TQ:TH], rows_sb[0:1, TQ:TH], AF.Exp, scale=-0.5))
                pm_state["rstd"] = act_flush()
            elif c == 13:
                wait("tensor", sact, pm_state["rstd"])
                wait("tensor", sv, pm_state["rows"])
                just("tensor", lambda e: e.matmul(ps[6][:, :], ones_sb[0:1, :],
                                                  rows_sb[0:1, 0:TQ],
                                                  start=True, stop=True))
                pm_state["bc"] = inc("tensor", smm, lambda e: e.matmul(
                    ps[7][:, :], ones_sb[0:1, :], rows_sb[0:1, TQ:TH],
                    start=True, stop=True))
                wait_settle("vector", smm, pm_state["bc"])
                just("vector", lambda e: e.tensor_copy(mean_sb[:, :], ps[6][:, :]))
                pm_state["cp"] = inc("vector", sv, lambda e: e.tensor_copy(
                    rstd_sb[:, :], ps[7][:, :]))

        def prefetch_ot(c):
            if c >= NT:
                return
            s_r, s_k, s_v = c % 2, 2 + c % 2, 4 + c % 2
            if c >= 2:
                slot_free_at[s_r] = r_grp[c - 2]
                slot_free_at[s_k] = k_grp[c - 2]
                slot_free_at[s_v] = v_grp[c - 2]
            load_slab("sync", wr[c], s_r)
            load_slab("gpsimd", wk[c], s_k)
            load_slab("gpsimd", wv[c], s_v)

        prefetch_ot(0)
        prefetch_ot(1)
        for c in range(NT):
            s_r, s_k, s_v = c % 2, 2 + c % 2, 4 + c % 2
            if c >= 2:
                wait("tensor", sact, kcp_done[c - 2])
                wait("tensor", sv, kv_done[c - 2])
            wait_slab("tensor", s_r)
            for kt in range(NT):
                if c < 2:
                    wait("tensor", sv, mix_ht[kt])
                if kt < NT - 1:
                    just("tensor", lambda e, k=kt, sl=s_r, cc=c, s=(kt == 0):
                         e.matmul(ps[0 + cc % 2][:, :],
                                  slab[sl][:, k * 128:(k + 1) * 128],
                                  xr_v[:, k * TQ:(k + 1) * TQ], start=s, stop=False))
                else:
                    r_grp[c] = inc("tensor", smm, lambda e, k=kt, sl=s_r, cc=c:
                                   e.matmul(ps[0 + cc % 2][:, :],
                                            slab[sl][:, k * 128:(k + 1) * 128],
                                            xr_v[:, k * TQ:(k + 1) * TQ],
                                            start=False, stop=True))
            wait_slab("tensor", s_k)
            for kt in range(NT):
                if c < 2:
                    wait("tensor", sv, mix_ht[kt])
                if kt < NT - 1:
                    just("tensor", lambda e, k=kt, sl=s_k, cc=c, s=(kt == 0):
                         e.matmul(ps[2 + cc % 2][:, :],
                                  slab[sl][:, k * 128:(k + 1) * 128],
                                  xk_v[:, k * TQ:(k + 1) * TQ], start=s, stop=False))
                else:
                    k_grp[c] = inc("tensor", smm, lambda e, k=kt, sl=s_k, cc=c:
                                   e.matmul(ps[2 + cc % 2][:, :],
                                            slab[sl][:, k * 128:(k + 1) * 128],
                                            xk_v[:, k * TQ:(k + 1) * TQ],
                                            start=False, stop=True))
            wait_slab("tensor", s_v)
            for kt in range(NT):
                if c < 2:
                    wait("tensor", sv, mix_ht[kt])
                if kt < NT - 1:
                    just("tensor", lambda e, k=kt, sl=s_v, cc=c, s=(kt == 0):
                         e.matmul(ps[4 + cc % 2][:, :],
                                  slab[sl][:, k * 128:(k + 1) * 128],
                                  xv_v[:, k * TQ:(k + 1) * TQ], start=s, stop=False))
                else:
                    v_grp[c] = inc("tensor", smm, lambda e, k=kt, sl=s_v, cc=c:
                                   e.matmul(ps[4 + cc % 2][:, :],
                                            slab[sl][:, k * 128:(k + 1) * 128],
                                            xv_v[:, k * TQ:(k + 1) * TQ],
                                            start=False, stop=True))
            prefetch_ot(c + 2)
            post_c(c)

            wait_settle("scalar", smm, r_grp[c])
            if c >= 2:
                wait("scalar", sv, wkv_done[c - 2])
            inc("scalar", sact, lambda e, cc=c: e.activation(
                r_buf[cc % 2][:, :], ps[0 + cc % 2][:, :], AF.Sigmoid))
            wait_settle("scalar", smm, k_grp[c])
            inc("scalar", sact, lambda e, cc=c: e.activation(
                k_sb[cc % 2][:, :], ps[2 + cc % 2][:, :], AF.Copy))
            sig_done[c] = kcp_done[c] = act_flush()

            wait("vector", sact, kcp_done[c])
            wait_settle("vector", smm, v_grp[c])
            if q == 0 and c == 0:
                wait("vector", sact, wcol_n)
            kv_done[c] = inc("vector", sv, lambda e, cc=c: e.tensor_tensor(
                kv_b[:, :], ps[4 + cc % 2][:, :], k_sb[cc % 2][:, :], OP.mult))
            just("vector", lambda e, cc=c: e.tensor_copy(S_b[:, 0:1],
                                                         state[:, cc:cc + 1]))
            just("vector", lambda e, cc=c: e.tensor_tensor_scan(
                S_b[:, 1:TQ + 1],
                broadcast_tensor_aps(w_c[:, cc:cc + 1], kv_b[:, :])[0],
                kv_b[:, :], S_b[:, 0:1], OP.mult, OP.add))
            just("vector", lambda e, cc=c: e.tensor_copy(state[:, cc:cc + 1],
                                                         S_b[:, TQ:TQ + 1]))
            just("vector", lambda e, cc=c: e.scalar_tensor_tensor(
                pre_b[:, :], kv_b[:, :], tf_c[:, cc:cc + 1], S_b[:, 0:TQ],
                OP.mult, OP.add))
            wait("vector", sact, sig_done[c])
            wkv_done[c] = inc("vector", sv, lambda e, cc=c: e.tensor_tensor(
                wkv_v[:, cc * TQ:(cc + 1) * TQ], pre_b[:, :], r_buf[cc % 2][:, :],
                OP.mult))

        wo_grp = [0] * NT
        wo_ev = [0] * NT

        def prefetch_wo(h):
            if h >= NT:
                return
            sl = h % 2
            if h < 2:
                slot_free_at[sl] = r_grp[NT - 2 + h]
            else:
                slot_free_at[sl] = wo_grp[h - 2]
            load_slab("sync", wo[h], sl)

        prefetch_wo(0)
        prefetch_wo(1)
        tc_h = q % 2
        sqh = [0] * NT
        h2_mm = [0] * NT

        def emit_ln2_stats(j):
            xs = x1[:, j * TH + qc0: j * TH + qc0 + TQ]
            wait("tensor", sv, wo_ev[j])
            just("tensor", lambda e, s=xs, st=(j == 0), t=tc_h: e.matmul(
                ps[4 + t][0:1, :], ones_sb[:, 0:1], s,
                start=st, stop=(j == NT - 1)))
            wait("tensor", sv, sqh[j])
            h2_mm[j] = inc("tensor", smm,
                           lambda e, jj=j, st=(j == 0), t=tc_h: e.matmul(
                               ps[2 + t][0:1, :], ones_sb[:, 0:1],
                               sq_b2[jj % 3][:, :],
                               start=st, stop=(j == NT - 1)))

        for h in range(NT):
            sl = h % 2
            wait_slab("tensor", sl)
            if h == 0:
                wait("tensor", sv, wkv_done[NT - 1])
                wait("tensor", sact, kcp_done[NT - 1])
            if h >= 2:
                wait("tensor", sv, wo_ev[h - 2])
            for ot in range(NT):
                if ot < NT - 1:
                    just("tensor", lambda e, o=ot, s=(ot == 0), hh=h, ss=sl:
                         e.matmul(ps[6 + hh % 2][:, :],
                                  slab[ss][:, o * 128:(o + 1) * 128],
                                  wkv_v[:, o * TQ:(o + 1) * TQ], start=s, stop=False))
                else:
                    wo_grp[h] = inc("tensor", smm, lambda e, o=ot, hh=h, ss=sl:
                                    e.matmul(ps[6 + hh % 2][:, :],
                                             slab[ss][:, o * 128:(o + 1) * 128],
                                             wkv_v[:, o * TQ:(o + 1) * TQ],
                                             start=False, stop=True))
            prefetch_wo(h + 2)
            wait_settle("vector", smm, wo_grp[h])
            xsl = x1[:, h * TH + qc0: h * TH + qc0 + TQ]
            wo_ev[h] = inc("vector", sv, lambda e, s=xsl, hh=h: e.tensor_tensor(
                s, s, ps[6 + hh % 2][:, :], OP.add))
            if h >= 3:
                wait("vector", smm, h2_mm[h - 3])
            sqh[h] = inc("vector", sv, lambda e, s=xsl, hh=h: e.tensor_tensor(
                sq_b2[hh % 3][:, :], s, s, OP.mult))
            if h >= 2:
                emit_ln2_stats(h - 2)
        for hh2 in (NT - 2, NT - 1):
            emit_ln2_stats(hh2)
        wait_settle("vector", smm, h2_mm[NT - 1])
        just("vector", lambda e, t=tc_h: e.tensor_scalar(
            m_row[0:1, t * TQ:(t + 1) * TQ], ps[4 + t][0:1, :], INV_D, None,
            OP.mult))
        stats_ev = inc("vector", sv, lambda e, t=tc_h: e.tensor_scalar(
            s_row[0:1, t * TQ:(t + 1) * TQ], ps[2 + t][0:1, :], INV_D, None,
            OP.mult))
        stats_done[q] = stats_ev

        slot_free_at[0] = wo_grp[NT - 2]
        slot_free_at[1] = wo_grp[NT - 1]
        slot_free_at[2] = k_grp[NT - 2]
        slot_free_at[3] = k_grp[NT - 1]
        slot_free_at[4] = v_grp[NT - 2]
        slot_free_at[5] = v_grp[NT - 1]

        premix.clear()
        if prefetch_next:
            premix.update({"for_q": q + 1, "xp": xp, "vgrp15": v_grp[NT - 1]})

    def emit_ffn(half, dbg_on=False):
        wait("vector", smm, smm.n)
        wait("vector", sact, sact.n)
        wait("tensor", sv, sv.n)
        wait("tensor", sact, sact.n)

        wait("vector", sv, stats_done[2 * half + 1])
        just("vector", lambda e: e.tensor_tensor(we_row[0:1, :], m_row[0:1, :],
                                                 m_row[0:1, :], OP.mult))
        just("vector", lambda e: e.tensor_tensor(s_row[0:1, :], s_row[0:1, :],
                                                 we_row[0:1, :], OP.subtract))
        rows_n = inc("vector", sv, lambda e: e.tensor_scalar(
            s_row[0:1, :], s_row[0:1, :], EPS, None, OP.add))
        wait("scalar", sv, rows_n)
        just("scalar", lambda e: e.activation(s_row[0:1, :], s_row[0:1, :],
                                              AF.Ln))
        inc("scalar", sact, lambda e: e.activation(
            s_row[0:1, :], s_row[0:1, :], AF.Exp, scale=-0.5))
        rstd_n = act_flush()
        wait("tensor", sact, rstd_n)
        wait("tensor", sv, rows_n)
        bcast_mm = 0
        for tc in range(2):
            just("tensor", lambda e, t=tc: e.matmul(
                ps[4 + t][:, :], ones_sb[0:1, :], m_row[0:1, t * TQ:(t + 1) * TQ],
                start=True, stop=True))
            bcast_mm = inc("tensor", smm, lambda e, t=tc: e.matmul(
                ps[6 + t][:, :], ones_sb[0:1, :], s_row[0:1, t * TQ:(t + 1) * TQ],
                start=True, stop=True))
        wait_settle("vector", smm, bcast_mm)
        ln2_n = 0
        ln2_ht = [0] * NT
        for ht in range(NT):
            for tc in range(2):
                xsl = x1[:, ht * TH + tc * TQ: ht * TH + tc * TQ + TQ]
                just("vector", lambda e, s=xsl, t=tc: e.tensor_tensor(
                    gs_b[0][:, :], s, ps[4 + t][:, :], OP.subtract))
                ln2_n = inc("vector", sv, lambda e, h=ht, t=tc: e.tensor_tensor(
                    ln2_v[:, h * TH + t * TQ: h * TH + (t + 1) * TQ],
                    gs_b[0][:, :], ps[6 + t][:, :], OP.mult))
            ln2_ht[ht] = ln2_n

        wrow_n = [0]

        def emit_router():
            wait("tensor", sv, ln2_n)
            wait("tensor", sgd, wrf_n)
            lg_mm = 0
            for tc in range(2):
                for ht in range(NT):
                    if ht < NT - 1:
                        just("tensor", lambda e, h=ht, t=tc, st=(ht == 0):
                             e.matmul(ps[t][0:8, :], wrf_sb[:, h * 8:(h + 1) * 8],
                                      ln2_v[:, h * TH + t * TQ: h * TH + (t + 1) * TQ],
                                      start=st, stop=False))
                    else:
                        lg_mm = inc("tensor", smm, lambda e, h=ht, t=tc:
                                    e.matmul(ps[t][0:8, :], wrf_sb[:, h * 8:(h + 1) * 8],
                                             ln2_v[:, h * TH + t * TQ:
                                                   h * TH + (t + 1) * TQ],
                                             start=False, stop=True))
            wait_settle("vector", smm, lg_mm)
            lgc_n = 0
            for tc in range(2):
                lgc_n = inc("vector", sv, lambda e, t=tc: e.tensor_copy(
                    lg_sb[0:8, t * TQ:(t + 1) * TQ], ps[t][0:8, :]))
            wait("tensor", sv, lgc_n)
            lt_mm = 0
            for tcol in range(8):
                if tcol < 7:
                    just("tensor", lambda e, t=tcol: e.transpose(
                        ps[2][:, t * 8:(t + 1) * 8], lg_sb[0:8, t * 128:(t + 1) * 128],
                        id_sb[0:8, 0:8]))
                else:
                    lt_mm = inc("tensor", smm, lambda e, t=tcol: e.transpose(
                        ps[2][:, t * 8:(t + 1) * 8], lg_sb[0:8, t * 128:(t + 1) * 128],
                        id_sb[0:8, 0:8]))
            wait_settle("vector", smm, lt_mm)
            for tcol in range(8):
                just("vector", lambda e, t=tcol: e.tensor_copy(
                    l8[:, :], ps[2][:, t * 8:(t + 1) * 8]))
                just("vector", lambda e: e.reduce_max(m1_b[:, :], l8[:, :], axis=AX.X))
                just("vector", lambda e: e.tensor_scalar(eq1_b[:, :], l8[:, :],
                                                         m1_b[:, 0:1], None, OP.is_equal))
                just("vector", lambda e: e.scalar_tensor_tensor(
                    w8[:, :], eq1_b[:, :], -1e9, l8[:, :], OP.mult, OP.add))
                just("vector", lambda e: e.reduce_max(m2_b[:, :], w8[:, :], axis=AX.X))
                just("vector", lambda e: e.tensor_scalar(eq2_b[:, :], w8[:, :],
                                                         m2_b[:, 0:1], None, OP.is_equal))
                n = inc("vector", sv, lambda e: e.tensor_tensor(
                    p1_b[:, :], m1_b[:, :], m2_b[:, :], OP.subtract))
                wait("scalar", sv, n)
                n2 = inc("scalar", sact, lambda e: e.activation(p1_b[:, :], p1_b[:, :],
                                                                AF.Sigmoid))
                wait("vector", sact, n2)
                just("vector", lambda e: e.tensor_scalar(p2_b[:, :], p1_b[:, :],
                                                         -1.0, 1.0, OP.mult, OP.add))
                just("vector", lambda e: e.tensor_scalar(t8[:, :], eq1_b[:, :],
                                                         p1_b[:, 0:1], None, OP.mult))
                wn = inc("vector", sv, lambda e: e.scalar_tensor_tensor(
                    w8[:, :], eq2_b[:, :], p2_b[:, 0:1], t8[:, :], OP.mult, OP.add))
                wait("tensor", sv, wn)
                if tcol == 4:
                    wait("tensor", sv, wrow_n[0])
                tn = inc("tensor", smm, lambda e, t=tcol: e.transpose(
                    ps[3][0:8, (t % 4) * 128:(t % 4 + 1) * 128], w8[:, :], id_sb[:, :]))
                if tcol == 3 or tcol == 7:
                    wait_settle("vector", smm, tn)
                    base = 0 if tcol == 3 else 512
                    wrow_n[0] = inc("vector", sv, lambda e, b=base: e.tensor_copy(
                        wr_sb[0:8, b:b + 512], ps[3][0:8, :]))

            if dbg_on:
                wait("gpsimd", sv, sv.n)
                inc("gpsimd", sgd, lambda e: e.dma_start(out=dbg_ln2, in_=bigA[:, 0:NT * TH]), 16)
                inc("gpsimd", sgd, lambda e: e.dma_start(out=dbg_wr, in_=wr_sb), 16)
                inc("gpsimd", sgd, lambda e: e.dma_start(out=dbg_lg, in_=lg_sb), 16)
                inc("gpsimd", sgd, lambda e: e.dma_start(out=dbg_sm, in_=arena[:, 5664:5760]), 16)
                holdr = sgd.n
                wait("tensor", sgd, holdr)


        g_grp = {}
        u_grp = {}
        d_grp = {}
        z_done = {}
        d_ev = {}
        sil_done = {}
        we_bufs = [(we_row, we_b), (we_row2, we_b2)]
        we_mm_par = [0, 0]
        we_cp_e = {}
        last_dev_for_ht = [0] * NT

        def pf_gu(e_, it):
            if it >= NT:
                e_, it = e_ + 1, it - NT
            if e_ >= NE:
                return
            sg, su = it % 2, 2 + it % 2
            if it >= 2:
                slot_free_at[sg] = g_grp[(e_, it - 2, 1)]
                slot_free_at[su] = u_grp[(e_, it - 2, 1)]
            elif e_ > 0:
                slot_free_at[sg] = g_grp[(e_ - 1, NT - 2 + it, 1)]
                slot_free_at[su] = u_grp[(e_ - 1, NT - 2 + it, 1)]
            load_slab("sync", wg[e_, it], sg)
            load_slab("sync", wu[e_, it], su)

        def pf_d(e_, ht):
            if ht >= NT:
                e_, ht = e_ + 1, ht - NT
            if e_ >= NE:
                return
            sd = 4 + ht % 2
            if ht >= 2:
                slot_free_at[sd] = d_grp[(e_, ht - 2, 1)]
            elif e_ > 0:
                slot_free_at[sd] = d_grp[(e_ - 1, NT - 2 + ht, 1)]
            load_slab("gpsimd", wd[e_, ht], sd)

        def emit_unit(e_, it, tc, early=False):
            sg, su = it % 2, 2 + it % 2
            unit = it * 2 + tc
            pg, pu = ps[unit % 2], ps[2 + unit % 2]
            if e_ == 0 and it < 2:
                wait("tensor", sv, rows_n)
            if e_ > 0 and unit <= 1:
                wait("tensor", sv, we_cp_e[e_])
            if it >= 1:
                wait("tensor", sv, z_done[(e_, it - 1, tc)])
            elif e_ > 0:
                wait("tensor", sv, z_done[(e_ - 1, NT - 1, tc)])
            wait_slab("tensor", sg)
            for kt in range(NT):
                if early:
                    wait("tensor", sv, ln2_ht[kt])
                if kt < NT - 1:
                    just("tensor",
                         lambda e, k=kt, s=(kt == 0), b=pg, ss=sg, t=tc:
                         e.matmul(b[:, :],
                                  slab[ss][:, k * 128:(k + 1) * 128],
                                  ln2_v[:, k * TH + t * TQ:
                                        k * TH + (t + 1) * TQ],
                                  start=s, stop=False))
                else:
                    g_grp[(e_, it, tc)] = inc(
                        "tensor", smm,
                        lambda e, k=kt, b=pg, ss=sg, t=tc:
                        e.matmul(b[:, :],
                                 slab[ss][:, k * 128:(k + 1) * 128],
                                 ln2_v[:, k * TH + t * TQ:
                                       k * TH + (t + 1) * TQ],
                                 start=False, stop=True))
            wait_slab("tensor", su)
            for kt in range(NT):
                if early:
                    wait("tensor", sv, ln2_ht[kt])
                if kt < NT - 1:
                    just("tensor",
                         lambda e, k=kt, s=(kt == 0), b=pu, ss=su, t=tc:
                         e.matmul(b[:, :],
                                  slab[ss][:, k * 128:(k + 1) * 128],
                                  ln2_v[:, k * TH + t * TQ:
                                        k * TH + (t + 1) * TQ],
                                  start=s, stop=False))
                else:
                    u_grp[(e_, it, tc)] = inc(
                        "tensor", smm,
                        lambda e, k=kt, b=pu, ss=su, t=tc:
                        e.matmul(b[:, :],
                                 slab[ss][:, k * 128:(k + 1) * 128],
                                 ln2_v[:, k * TH + t * TQ:
                                       k * TH + (t + 1) * TQ],
                                 start=False, stop=True))
            wait_settle("scalar", smm, g_grp[(e_, it, tc)])
            if it >= 1:
                wait("scalar", sv, z_done[(e_, it - 1, tc)])
            elif e_ > 0:
                wait("scalar", sv, z_done[(e_ - 1, NT - 1, tc)])
            inc("scalar", sact, lambda e, u=unit: e.activation(
                gs_b[u % 2][:, :], ps[u % 2][:, :], AF.Silu))
            sil_done[(e_, it, tc)] = act_flush()
            wait("vector", sact, sil_done[(e_, it, tc)])
            wait_settle("vector", smm, u_grp[(e_, it, tc)])
            just("vector", lambda e, u=unit: e.tensor_scalar(
                gs_b[u % 2][:, :], gs_b[u % 2][:, :], -CLIP, CLIP,
                OP.max, OP.min))
            z_done[(e_, it, tc)] = inc("vector", sv,
                                       lambda e, u=unit, b=pu, h=it, t=tc:
                                       e.tensor_tensor(
                z_v[:, h * TH + t * TQ: h * TH + (t + 1) * TQ],
                gs_b[u % 2][:, :], b[:, :], OP.mult))

        pf_gu(0, 0)
        pf_gu(0, 1)
        pf_d(0, 0)
        pf_d(0, 1)

        for it0 in range(NT):
            for tc0 in (0, 1):
                emit_unit(0, it0, tc0, early=(it0 < 4))
            pf_gu(0, it0 + 2)
        emit_router()

        def emit_we_dma(e_):
            wrx = we_bufs[e_ % 2][0]
            wait_settle("scalar", sv, wrow_n[0])
            wait("scalar", smm, we_mm_par[e_ % 2])
            return inc("scalar", ssc, lambda e, x=e_, w=wrx: e.dma_start(
                out=w[0:1, :], in_=wr_sb[x:x + 1, :]), 16)

        def emit_we_bcast(e_, wrdma, pa, pb):
            wrx, wbx = we_bufs[e_ % 2]
            wait("tensor", ssc, wrdma)
            just("tensor", lambda e, p=pa, w=wrx: e.matmul(
                p[:, :], ones_sb[0:1, :], w[0:1, 0:TQ], start=True, stop=True))
            we_mm_par[e_ % 2] = inc("tensor", smm, lambda e, p=pb, w=wrx:
                                    e.matmul(p[:, :], ones_sb[0:1, :],
                                             w[0:1, TQ:TH],
                                             start=True, stop=True))
            wait_settle("vector", smm, we_mm_par[e_ % 2])
            just("vector", lambda e, p=pa, w=wbx: e.tensor_copy(w[:, 0:TQ],
                                                                p[:, :]))
            we_cp_e[e_] = inc("vector", sv, lambda e, p=pb, w=wbx:
                              e.tensor_copy(w[:, TQ:TH], p[:, :]))

        pend_wrdma = [0]
        for e_ in range(NE):
            if e_ == 0:
                emit_we_bcast(0, emit_we_dma(0), ps[6], ps[7])
            we_cur = we_bufs[e_ % 2][1]
            for it in range(0 if e_ > 0 else NT, NT):
                for tc in range(2):
                    emit_unit(e_, it, tc)
                pf_gu(e_, it + 2)
            for ht in range(NT):
                sd = 4 + ht % 2
                wait_slab("tensor", sd)
                for tc in range(2):
                    gi = ht * 2 + tc
                    pd = ps[4 + gi % 4]
                    if gi >= 4:
                        wait("tensor", sv, d_ev[(e_, (gi - 4) // 2, (gi - 4) % 2)])
                    elif e_ > 0:
                        wait("tensor", sv,
                             d_ev[(e_ - 1, (28 + gi) // 2, (28 + gi) % 2)])
                    else:
                        wait("tensor", sv, we_cp_e[0])
                    if gi == 0:
                        wait("tensor", sv, z_done[(e_, NT - 1, 1)])
                    for kt in range(NT):
                        if kt < NT - 1:
                            just("tensor",
                                 lambda e, k=kt, s=(kt == 0), b=pd, ss=sd, t=tc:
                                 e.matmul(b[:, :],
                                          slab[ss][:, k * 128:(k + 1) * 128],
                                          z_v[:, k * TH + t * TQ:
                                              k * TH + (t + 1) * TQ],
                                          start=s, stop=False))
                        else:
                            d_grp[(e_, ht, tc)] = inc(
                                "tensor", smm,
                                lambda e, k=kt, b=pd, ss=sd, t=tc:
                                e.matmul(b[:, :],
                                         slab[ss][:, k * 128:(k + 1) * 128],
                                         z_v[:, k * TH + t * TQ:
                                             k * TH + (t + 1) * TQ],
                                         start=False, stop=True))
                    wait_settle("vector", smm, d_grp[(e_, ht, tc)])
                    xsl = x1[:, ht * TH + tc * TQ: ht * TH + tc * TQ + TQ]
                    just("vector", lambda e, b=pd, t=tc, w=we_cur:
                         e.tensor_tensor(d_b[:, :], b[:, :],
                                         w[:, t * TQ:(t + 1) * TQ], OP.mult))
                    d_ev[(e_, ht, tc)] = inc(
                        "vector", sv, lambda e, s=xsl: e.tensor_tensor(
                            s, s, d_b[:, :], OP.add))
                    if e_ == NE - 1 and tc == 1:
                        last_dev_for_ht[ht] = d_ev[(e_, ht, tc)]
                pf_d(e_, ht + 2)
                if e_ < NE - 1:
                    if ht == 0:
                        pend_wrdma[0] = emit_we_dma(e_ + 1)
                    elif ht == 2:
                        emit_we_bcast(e_ + 1, pend_wrdma[0], ps[2], ps[3])

        eL = NE - 1
        slot_free_at[0] = g_grp[(eL, NT - 2, 1)]
        slot_free_at[1] = g_grp[(eL, NT - 1, 1)]
        slot_free_at[2] = u_grp[(eL, NT - 2, 1)]
        slot_free_at[3] = u_grp[(eL, NT - 1, 1)]
        slot_free_at[4] = d_grp[(eL, NT - 2, 1)]
        slot_free_at[5] = d_grp[(eL, NT - 1, 1)]

        for ht in range(NT):
            wait_settle("gpsimd", sv, last_dev_for_ht[ht])
            inc("gpsimd", sgd, lambda e, h=ht: e.dma_start(
                out=y[half, h], in_=x1[:, h * TH:(h + 1) * TH]), 16)
        out_dma_done[half] = sgd.n

    emit_quarter(0, prefetch_next=True)
    if debug:
        wait("gpsimd", sv, sv.n)
        inc("gpsimd", sgd, lambda e: e.dma_start(out=dbg_xk, in_=bigA[:, 0:NT * TQ * 3]), 16)
        inc("gpsimd", sgd, lambda e: e.dma_start(out=dbg_wkv, in_=bigA[:, 3 * NT * TQ:4 * NT * TQ]), 16)
        hold0 = sgd.n
        wait("tensor", sgd, hold0)
    emit_quarter(1)
    if debug:
        wait("gpsimd", sv, sv.n)
        inc("gpsimd", sgd, lambda e: e.dma_start(out=dbg_x1, in_=x1[:, :]), 16)
        hold1 = sgd.n
        wait("tensor", sgd, hold1)
    emit_ffn(0, debug)
    emit_quarter(2, prefetch_next=True)
    emit_quarter(3)
    emit_ffn(1, False)
    wait("gpsimd", sgd, sgd.n)

    with ExitStack() as es2:
        for c in ctrs:
            c.sem = es2.enter_context(nc.semaphore(c.name))
        block = es2.enter_context(nc.Block())

        @block.sync
        def _(sync):
            for f in plan["sync"]:
                f(sync)

        @block.tensor
        def _(t):
            for f in plan["tensor"]:
                f(t)

        @block.vector
        def _(v):
            for f in plan["vector"]:
                f(v)

        @block.scalar
        def _(s):
            for f in plan["scalar"]:
                f(s)

        @block.gpsimd
        def _(g):
            for f in plan["gpsimd"]:
                f(g)

    es.close()
    return nc



def _pretile(weff):
    K, M = weff.shape
    w4 = weff.reshape(K // 128, 128, M // 128, 128)
    return np.ascontiguousarray(w4.transpose(2, 1, 0, 3)).reshape(
        M // 128, 128, (K // 128) * 128)


def _col16(v):
    return np.ascontiguousarray(v.reshape(16, 128).T)


def kernel(L0_attn_hidden_states, L0_attn_Wr, L0_attn_Wk, L0_attn_Wv, L0_attn_Wo,
           L0_attn_time_decay, L0_attn_time_first, L0_attn_time_mix_k,
           L0_attn_time_mix_v, L0_attn_time_mix_r, L0_ffn_Wr, L0_ffn_Wg,
           L0_ffn_Wu, L0_ffn_Wd):
    x = np.asarray(L0_attn_hidden_states, dtype=np.float32)
    B = x.shape[0]
    f32 = lambda a: np.asarray(a, dtype=np.float32)

    bf16 = lambda a: np.ascontiguousarray(a.astype(ml_dtypes.bfloat16))
    shared = {
        "wr": bf16(_pretile(f32(L0_attn_Wr).T)),
        "wk": bf16(_pretile(f32(L0_attn_Wk).T)),
        "wv": bf16(_pretile(f32(L0_attn_Wv).T)),
        "wo": bf16(_pretile(f32(L0_attn_Wo).T)),
        "wg": bf16(np.stack([_pretile(f32(L0_ffn_Wg)[e]) for e in range(NE)])),
        "wu": bf16(np.stack([_pretile(f32(L0_ffn_Wu)[e]) for e in range(NE)])),
        "wd": bf16(np.stack([_pretile(f32(L0_ffn_Wd)[e]) for e in range(NE)])),
        "wrf": bf16(np.ascontiguousarray(
            f32(L0_ffn_Wr).reshape(16, 128, 8).transpose(1, 0, 2).reshape(128, 128))),
        "tmk": _col16(f32(L0_attn_time_mix_k).reshape(-1)),
        "tmv": _col16(f32(L0_attn_time_mix_v).reshape(-1)),
        "tmr": _col16(f32(L0_attn_time_mix_r).reshape(-1)),
        "td": _col16(f32(L0_attn_time_decay).reshape(-1)),
        "tf": _col16(f32(L0_attn_time_first).reshape(-1)),
        "ident": np.eye(128, dtype=np.float32),
        "onesd": np.ones((128, 128), dtype=np.float32),
    }

    nc = build_nc()
    in_maps = []
    for b in range(B):
        m = dict(shared)
        m["xT"] = np.ascontiguousarray(x[b].T).reshape(NT, 128, 2048)
        in_maps.append(m)

    res = run_bass_kernel_spmd(nc, in_maps, core_ids=list(range(N_CORES)))

    out = np.empty((B, 2048, 2048), dtype=np.float32)
    for b in range(B):
        arr = res.results[b]["y"]
        for half in range(2):
            out[b, half * TH:(half + 1) * TH, :] = arr[half].reshape(2048, TH).T
    return out
